# version 07 (final)
# baseline (speedup 1.0000x reference)
import contextlib
import numpy as np
import ml_dtypes
import concourse.bass as bass
import concourse.mybir as mybir
from concourse.bass_utils import run_bass_kernel_spmd

F32 = mybir.dt.float32
BF16 = mybir.dt.bfloat16
AF = mybir.ActivationFunctionType
ALU = mybir.AluOpType
AX = mybir.AxisListType
BF = ml_dtypes.bfloat16

P = 128
NCORE = 8
D = 1024
NCH = D // P
B, S = 2, 8192
TOK = B * S
TPC = TOK // NCORE
NT = TPC // P
NEG = -1.0e9

L0_SLOT_EXT = [16, 32, 48, 64]
L0_TT = lambda i: [i, 7 - i, 8 + i, 15 - i]


class Prog:
    ENGS = ("tensor", "vector", "scalar", "sync", "gpsimd")

    def __init__(self):
        self.phases = [[]]

    def on(self, eng, fn, dma=False):
        self.phases[-1].append((eng, fn, dma))

    def te(self, fn): self.on("tensor", fn)
    def ve(self, fn): self.on("vector", fn)
    def se(self, fn): self.on("scalar", fn)
    def sy(self, fn): self.on("sync", fn, dma=True)
    def gp(self, fn): self.on("gpsimd", fn, dma=True)
    def se_dma(self, fn): self.on("scalar", fn, dma=True)

    def bar(self):
        if self.phases[-1]:
            self.phases.append([])

    def raw(self, eng, fn):
        self.phases[-1].append((eng, fn, "raw"))

    def build(self, block, sem):
        self.bar()
        phases = [ph for ph in self.phases if ph]
        cums, cum = [], 0
        for ph in phases:
            engs = set()
            for eng, fn, dma in ph:
                if dma == "raw":
                    continue
                if dma:
                    cum += 16
                else:
                    engs.add(eng)
            cum += len(engs)
            cums.append(cum)
        total = cum

        def mk_body(name):
            def body(eng):
                for pi, ph in enumerate(phases):
                    mine = [(fn, dma) for e, fn, dma in ph if e == name]
                    if not mine:
                        continue
                    if pi > 0:
                        eng.wait_ge(sem, cums[pi - 1])
                    last_c = max((i for i, (_, d) in enumerate(mine) if d is False), default=None)
                    for i, (fn, dma) in enumerate(mine):
                        inst = fn(eng)
                        if dma == "raw":
                            continue
                        if dma:
                            inst.then_inc(sem, 16)
                        elif i == last_c:
                            inst.then_inc(sem, 1)
                eng.wait_ge(sem, total)
            return body

        used = {e for ph in phases for (e, _, _) in ph}
        for name in self.ENGS:
            if name in used:
                getattr(block, name)(mk_body(name))
        return total


def ceil_div(a, b):
    return -(-a // b)


def emit_ln(p, x_tile, xn_out, sq_scr, mcol, vcol, rcol, sqrow, epscol, mhalf):
    p.ve(lambda e: e.reduce_sum(mcol[:], x_tile, AX.X))
    p.se(lambda e: e.activation(sq_scr[:], x_tile, AF.Square, accum_out=sqrow[:]))
    p.bar()
    p.ve(lambda e: e.tensor_scalar_mul(mcol[:], mcol[:], 1.0 / D))
    p.bar()
    p.ve(lambda e: e.tensor_tensor(vcol[:], mcol[:], mcol[:], ALU.mult))
    p.bar()
    p.ve(lambda e: e.scalar_tensor_tensor(vcol[:], sqrow[:], 1.0 / D, vcol[:], ALU.mult, ALU.subtract))
    p.bar()
    p.se(lambda e: e.activation(rcol[:], vcol[:], AF.Ln, bias=epscol[:]))
    p.bar()
    p.se(lambda e: e.activation(rcol[:], rcol[:], AF.Exp, scale=mhalf[:]))
    p.bar()
    p.ve(lambda e: e.tensor_scalar(xn_out, x_tile, mcol[:], rcol[:], ALU.subtract, ALU.mult))
    p.bar()


def emit_cast_load(p, dst_sb, src_d, rows, cols, row_chunk=P):
    for r in range(ceil_div(rows, row_chunk)):
        p.gp(lambda e, r=r: e.dma_start(
            out=dst_sb[:, r * cols:(r + 1) * cols],
            in_=src_d[r * row_chunk:(r + 1) * row_chunk, :]))




def emit_ln_stream_to_fm(p, nc_unused, x_d, ntiles, xt, lnbufs, xn_fm, xnscr_d,
                         xn_tm_out=None, extra_tile_cb=None):
    sq = lnbufs["sq"]
    eps, mhalf = lnbufs["eps"], lnbufs["mhalf"]
    ncols = ntiles * P
    for t in range(ntiles):
        par = t % 2
        cur = xt[par]
        mcol, vcol, rcol, sqrow = lnbufs["cols"][par]
        xnb = lnbufs["xn_bf"][par]
        if t + 1 < ntiles:
            p.sy(lambda e, t=t: e.dma_start(out=xt[(t + 1) % 2][:], in_=x_d[(t + 1) * P:(t + 2) * P, :]))
        p.ve(lambda e, mcol=mcol, cur=cur: e.reduce_sum(mcol[:], cur[:], AX.X))
        p.se(lambda e, sqrow=sqrow, cur=cur: e.activation(sq[:], cur[:], AF.Square, accum_out=sqrow[:]))
        p.bar()
        p.ve(lambda e, mcol=mcol: e.tensor_scalar_mul(mcol[:], mcol[:], 1.0 / D))
        p.bar()
        p.ve(lambda e, mcol=mcol, vcol=vcol: e.tensor_tensor(vcol[:], mcol[:], mcol[:], ALU.mult))
        p.bar()
        p.ve(lambda e, vcol=vcol, sqrow=sqrow: e.scalar_tensor_tensor(vcol[:], sqrow[:], 1.0 / D, vcol[:], ALU.mult, ALU.subtract))
        p.bar()
        p.se(lambda e, rcol=rcol, vcol=vcol: e.activation(rcol[:], vcol[:], AF.Ln, bias=eps[:]))
        p.bar()
        p.se(lambda e, rcol=rcol: e.activation(rcol[:], rcol[:], AF.Exp, scale=mhalf[:]))
        p.bar()
        p.ve(lambda e, xnb=xnb, mcol=mcol, rcol=rcol, cur=cur: e.tensor_scalar(xnb[:], cur[:], mcol[:], rcol[:], ALU.subtract, ALU.mult))
        p.bar()
        p.se_dma(lambda e, t=t, xnb=xnb: e.dma_start(out=xnscr_d[t * P:(t + 1) * P, :], in_=xnb[:]))
        if xn_tm_out is not None:
            p.se_dma(lambda e, t=t, xnb=xnb: e.dma_start(out=xn_tm_out[t * P:(t + 1) * P, :], in_=xnb[:]))
        if extra_tile_cb is not None:
            extra_tile_cb(t, xnb)
        p.bar()
    for c in range(NCH):
        p.sy(lambda e, c=c: e.dma_start_transpose(xn_fm[c][:, 0:ncols], xnscr_d[0:ncols, c * P:(c + 1) * P]))
    p.bar()

def build_b0a():
    nc = bass.Bass("TRN2", target_bir_lowering=False)
    x_d = nc.dram_tensor("x", [TPC, D], F32, kind="ExternalInput").ap()
    wcd_d = nc.dram_tensor("Wc_down", [D, 512], F32, kind="ExternalInput").ap()
    wqu_d = nc.dram_tensor("Wq_up", [512, D], F32, kind="ExternalInput").ap()
    wkv_d = nc.dram_tensor("Wkv_up", [512, 128], F32, kind="ExternalInput").ap()
    ident_d = nc.dram_tensor("ident", [P, P], BF16, kind="ExternalInput").ap()
    xnT_d = nc.dram_tensor("xnT", [D, TPC], BF16, kind="ExternalOutput").ap()
    qT_d = nc.dram_tensor("qT", [D, TPC], BF16, kind="ExternalOutput").ap()
    kv_d = nc.dram_tensor("kv", [TPC, 128], BF16, kind="ExternalOutput").ap()
    xnscr_d = nc.dram_tensor("xnscr", [TPC, D], BF16, kind="Internal").ap()

    p = Prog()
    with contextlib.ExitStack() as ctx:
        sb = lambda n, s_, dt=F32: ctx.enter_context(nc.sbuf_tensor(n, s_, dt))
        psm = lambda n, s_, dt=F32: ctx.enter_context(nc.psum_tensor(n, s_, dt))
        xt = [sb(f"xt{i}", [P, D]) for i in range(2)]
        sq = sb("sq", [P, D])
        lncols = [[sb(f"ln{i}{n}", [P, 1]) for n in ("mc", "vc", "rc", "sr")] for i in range(2)]
        epscol = sb("epscol", [P, 1])
        mhalf = sb("mhalf", [P, 1])
        xn_bf2 = [sb(f"xnbf{i}", [P, D], BF16) for i in range(2)]
        xn_fm = [sb(f"xnfm{c}", [P, TPC], BF16) for c in range(NCH)]
        lat_fm = [sb(f"latfm{c}", [P, TPC], BF16) for c in range(4)]
        wcd = sb("wcd", [P, NCH * 512], BF16)
        wqu = sb("wqu", [P, 4 * D], BF16)
        wkv = sb("wkv", [P, 4 * 128], BF16)
        out_bf = sb("outbf", [P, TPC], BF16)
        kv_bf = sb("kvbf", [P, 128], BF16)
        pk = psm("pk", [P, 4 * 512])
        pkv = psm("pkv", [P, 128])
        sem = ctx.enter_context(nc.semaphore("sem"))
        block = ctx.enter_context(nc.Block())

        emit_cast_load(p, wcd, wcd_d, D, 512)
        emit_cast_load(p, wqu, wqu_d, 512, D)
        emit_cast_load(p, wkv, wkv_d, 512, 128)
        p.sy(lambda e: e.dma_start(out=xt[0][:], in_=x_d[0:P, :]))
        p.ve(lambda e: e.memset(epscol[:], 1e-5))
        p.ve(lambda e: e.memset(mhalf[:], -0.5))
        p.bar()

        lnbufs = dict(sq=sq, cols=lncols, eps=epscol, mhalf=mhalf, xn_bf=xn_bf2)
        emit_ln_stream_to_fm(p, None, x_d, NT, xt, lnbufs, xn_fm, xnscr_d)
        for c in range(NCH):
            p.se_dma(lambda e, c=c: e.dma_start(out=xnT_d[c * P:(c + 1) * P, :], in_=xn_fm[c][:]))
        p.bar()

        for lc in range(4):
            for dc in range(NCH):
                for tt in range(4):
                    p.te(lambda e, lc=lc, dc=dc, tt=tt: e.matmul(
                        pk[:, tt * 512:(tt + 1) * 512],
                        wcd[:, dc * 512 + lc * P: dc * 512 + lc * P + P],
                        xn_fm[dc][:, tt * 512:(tt + 1) * 512],
                        start=(dc == 0), stop=(dc == NCH - 1)))
            p.bar()
            for tt in range(4):
                p.se(lambda e, lc=lc, tt=tt: e.copy(lat_fm[lc][:, tt * 512:(tt + 1) * 512],
                                                    pk[:, tt * 512:(tt + 1) * 512]))
            p.bar()

        for qc in range(NCH):
            for lc in range(4):
                for tt in range(4):
                    p.te(lambda e, qc=qc, lc=lc, tt=tt: e.matmul(
                        pk[:, tt * 512:(tt + 1) * 512],
                        wqu[:, lc * D + qc * P: lc * D + qc * P + P],
                        lat_fm[lc][:, tt * 512:(tt + 1) * 512],
                        start=(lc == 0), stop=(lc == 3)))
            p.bar()
            for tt in range(4):
                p.se(lambda e, tt=tt: e.copy(out_bf[:, tt * 512:(tt + 1) * 512], pk[:, tt * 512:(tt + 1) * 512]))
            p.bar()
            p.se_dma(lambda e, qc=qc: e.dma_start(out=qT_d[qc * P:(qc + 1) * P, :], in_=out_bf[:]))
            p.bar()

        for t in range(NT):
            for lc in range(4):
                p.te(lambda e, t=t, lc=lc: e.matmul(
                    pkv[:], lat_fm[lc][:, t * P:(t + 1) * P], wkv[:, lc * 128:(lc + 1) * 128],
                    start=(lc == 0), stop=(lc == 3)))
            p.bar()
            p.ve(lambda e: e.tensor_copy(kv_bf[:], pkv[:]))
            p.bar()
            p.se_dma(lambda e, t=t: e.dma_start(out=kv_d[t * P:(t + 1) * P, :], in_=kv_bf[:]))
            p.bar()

        p.build(block, sem)
    return nc


def build_b0b():
    nc = bass.Bass("TRN2", target_bir_lowering=False)
    qT_d = nc.dram_tensor("qT", [D, TPC], BF16, kind="ExternalInput").ap()
    kT2_d = nc.dram_tensor("kT2", [P, S], BF16, kind="ExternalInput").ap()
    vaug_d = nc.dram_tensor("vaug", [64, P, 65], BF16, kind="ExternalInput").ap()
    masks_d = nc.dram_tensor("masks", [4, 16, P, 512], BF16, kind="ExternalInput").ap()
    wcp_d = nc.dram_tensor("Wc_proj", [D, D], F32, kind="ExternalInput").ap()
    ones_d = nc.dram_tensor("ones_row", [1, P], BF16, kind="ExternalInput").ap()
    s_d = nc.dram_tensor("s", [TPC, D], F32, kind="ExternalOutput").ap()

    p = Prog()
    with contextlib.ExitStack() as ctx:
        sb = lambda n, s_, dt=F32: ctx.enter_context(nc.sbuf_tensor(n, s_, dt))
        psm = lambda n, s_, dt=F32: ctx.enter_context(nc.psum_tensor(n, s_, dt))
        q_sb = [sb(f"q{c}", [P, TPC], BF16) for c in range(NCH)]
        kT2 = sb("kT2_s", [P, S], BF16)
        vaug = sb("vaug_s", [P, 64 * 65], BF16)
        masks = sb("masks_s", [P, 64 * 512], BF16)
        wcp = sb("wcp", [P, NCH * D], BF16)
        ones_row = sb("ones_s", [1, P], BF16)
        Pexp = sb("Pexp", [P, 4 * 512], BF16)
        sqs = sb("sqs", [P, 512], BF16)
        norm_ps = sb("norm_ps", [1, 1])
        qmax = sb("qmax", [1, 1])
        kmax = sb("kmax", [1, 1])
        cneg = sb("cneg", [1, 1])
        cneg_bf = sb("cneg_bf", [1, 1], BF16)
        ncol = sb("ncol", [P, 1])
        s0125 = sb("s0125", [P, 1])
        onescol = sb("onescol", [P, 1], BF16)
        rrow = sb("rrow", [1, 512])
        rbf = sb("rbf", [1, 512], BF16)
        rb_sb = sb("rb_sb", [64, 512])
        y_fm = sb("y_fm", [P, NCH * 512], BF16)
        s_f32 = sb("s_f32", [P, D])
        pscG = psm("pscG", [P, 4 * 512])
        pavG = psm("pavG", [65, 4 * 512])
        sem = ctx.enter_context(nc.semaphore("sem"))
        spe2 = ctx.enter_context(nc.semaphore("spe2"))
        sac2 = ctx.enter_context(nc.semaphore("sac2"))
        smk2 = ctx.enter_context(nc.semaphore("smk2"))
        snm2 = ctx.enter_context(nc.semaphore("snm2"))
        block = ctx.enter_context(nc.Block())

        for c in range(NCH):
            p.sy(lambda e, c=c: e.dma_start(out=q_sb[c][:], in_=qT_d[c * P:(c + 1) * P, :]))
        p.sy(lambda e: e.dma_start(out=kT2[:], in_=kT2_d[:]))
        p.sy(lambda e: e.dma_start(out=vaug[:].rearrange("p (k c) -> p k c", k=64), in_=vaug_d.rearrange("k p c -> p k c")))
        p.sy(lambda e: e.dma_start(out=masks[:].rearrange("p (m c) -> p m c", m=64), in_=masks_d.rearrange("m j p c -> p (m j) c")))
        p.sy(lambda e: e.dma_start(out=ones_row[:], in_=ones_d[:]))
        emit_cast_load(p, wcp, wcp_d, D, D)
        p.bar()
        p.ve(lambda e: e.memset(onescol[:], 1.0))
        p.ve(lambda e: e.memset(s0125[:], 0.125))
        p.bar()

        pn = pscG[0:1, 0:512]
        first = [True]
        def qk_norm(src_rows, dst_max, ncols, lo):
            for sl in range(ncols // 512):
                p.se(lambda e, sl=sl: e.activation(sqs[0:lo, :], src_rows[0:lo, sl * 512:(sl + 1) * 512], AF.Square))
                p.bar()
                p.te(lambda e, lo=lo: e.matmul(pn, onescol[0:lo, :], sqs[0:lo, :], start=True, stop=True))
                p.bar()
                p.ve(lambda e: e.reduce_max(norm_ps[:], pn, AX.X))
                p.bar()
                if first[0]:
                    p.ve(lambda e: e.tensor_copy(dst_max[:], norm_ps[:]))
                    first[0] = False
                else:
                    p.ve(lambda e: e.tensor_max(dst_max[:], dst_max[:], norm_ps[:]))
                p.bar()
        for c in range(NCH):
            first[0] = True if c == 0 else False
            qk_norm(q_sb[c][:], qmax, TPC, P)
        first[0] = True
        qk_norm(kT2[:], kmax, S, 64)
        p.ve(lambda e: e.tensor_tensor(cneg[:], qmax[:], kmax[:], ALU.mult))
        p.bar()
        p.se(lambda e: e.activation(cneg[:], cneg[:], AF.Sqrt))
        p.bar()
        p.ve(lambda e: e.tensor_scalar_mul(cneg[:], cneg[:], -0.125))
        p.ve(lambda e: e.tensor_copy(cneg_bf[:], cneg[:]))
        p.bar()
        pb1 = pscG[0:P, 512:513]
        p.te(lambda e: e.matmul(pb1, ones_row[:], cneg_bf[:], start=True, stop=True))
        p.bar()
        p.ve(lambda e: e.tensor_copy(ncol[:], pb1))
        p.bar()

        hcnt = {"spe": 0, "sac": 0, "smk": 0, "snm": 0}
        sac_hist = {}
        pav_free_at = {0: 0, 1: 0}
        for sl in range(4):
            nkb = L0_SLOT_EXT[sl]
            npair = nkb // 2
            for h in range(16):
                ch, ro = h // 2, (h % 2) * 64
                q2 = h % 2
                def emit_scores(j):
                    kb0, kb1 = 2 * j, 2 * j + 1
                    setq = j % 2
                    pset = pscG[:, setq * 1024:(setq + 1) * 1024]
                    def scores(e, j=j, kb0=kb0, kb1=kb1, sl=sl, ch=ch, ro=ro, pset=pset,
                               hj=(sl, h)):
                        if (hj, j - 2) in sac_hist:
                            e.wait_ge(sac2, sac_hist[(hj, j - 2)])
                        inst = None
                        for ki, kb in enumerate((kb0, kb1)):
                            inst = e.matmul(
                                pset[:, ki * 512:(ki + 1) * 512],
                                kT2[ro:ro + 64, kb * P:(kb + 1) * P],
                                q_sb[ch][ro:ro + 64, sl * 512:(sl + 1) * 512],
                                start=True, stop=True,
                                tile_position=(64, 0) if ro else None)
                        inst.then_inc(spe2, 1)
                        return inst
                    p.raw("tensor", scores)
                    hcnt["spe"] += 1
                    return hcnt["spe"]

                spe_of = {0: emit_scores(0)}
                for j in range(npair):
                    kb0, kb1 = 2 * j, 2 * j + 1
                    masked = (kb0 >= nkb - 16)
                    setq = j % 2
                    pset = pscG[:, setq * 1024:(setq + 1) * 1024]
                    pex = Pexp[:, setq * 1024:(setq + 1) * 1024]
                    if j + 1 < npair:
                        spe_of[j + 1] = emit_scores(j + 1)
                    spe_j = spe_of[j]
                    if masked:
                        j2 = kb0 - (nkb - 16)
                        mkbase = (sl * 16 + j2) * 512
                        def maskop(e, mkbase=mkbase, spe_j=spe_j, pset=pset):
                            e.wait_ge(spe2, spe_j)
                            return e.tensor_tensor(pset, pset, masks[:, mkbase:mkbase + 1024],
                                                   ALU.add).then_inc(smk2, 1)
                        p.raw("vector", maskop)
                        hcnt["smk"] += 1
                    smk_j = hcnt["smk"]
                    def expop(e, masked=masked, spe_j=spe_j, smk_j=smk_j, pset=pset, pex=pex):
                        if masked:
                            e.wait_ge(smk2, smk_j)
                        else:
                            e.wait_ge(spe2, spe_j)
                        return e.activation(pex, pset, AF.Exp, scale=s0125[:],
                                            bias=ncol[:]).then_inc(sac2, 1)
                    p.raw("scalar", expop)
                    hcnt["sac"] += 1
                    sac_hist[((sl, h), j)] = hcnt["sac"]
                    def avop(e, j=j, kb0=kb0, kb1=kb1, npair=npair, q2=q2, pex=pex,
                             sac_n=hcnt["sac"], pfree=pav_free_at[q2]):
                        e.wait_ge(sac2, sac_n)
                        if j == 0 and pfree:
                            e.wait_ge(snm2, pfree)
                        inst = None
                        for ki, kb in enumerate((kb0, kb1)):
                            inst = e.matmul(
                                pavG[:, q2 * 512:(q2 + 1) * 512],
                                vaug[:, kb * 65:(kb + 1) * 65],
                                pex[:, ki * 512:(ki + 1) * 512],
                                start=(kb == 0), stop=(kb == 2 * npair - 1))
                        if j == npair - 1:
                            inst.then_inc(spe2, 1)
                        return inst
                    p.raw("tensor", avop)
                hcnt["spe"] += 1
                av_done = hcnt["spe"]
                def nrm1(e, q2=q2, av_done=av_done):
                    e.wait_ge(spe2, av_done)
                    e.reciprocal(rrow[:], pavG[64:65, q2 * 512:q2 * 512 + 512])
                    return e.tensor_copy(rbf[:], rrow[:]).then_inc(snm2, 1)
                p.raw("vector", nrm1)
                hcnt["snm"] += 1
                n1 = hcnt["snm"]
                def nrm2(e, n1=n1):
                    e.wait_ge(snm2, n1)
                    return e.matmul(pavG[0:64, 2 * 512:2 * 512 + 512], ones_row[0:1, 0:64],
                                    rbf[:], start=True, stop=True).then_inc(spe2, 1)
                p.raw("tensor", nrm2)
                hcnt["spe"] += 1
                n2 = hcnt["spe"]
                co = ch * 512
                def nrm3(e, n2=n2, q2=q2, ro=ro, co=co):
                    e.wait_ge(spe2, n2)
                    e.tensor_copy(rb_sb[:], pavG[0:64, 2 * 512:2 * 512 + 512])
                    return e.tensor_tensor(y_fm[ro:ro + 64, co:co + 512],
                                           pavG[0:64, q2 * 512:q2 * 512 + 512],
                                           rb_sb[:], ALU.mult).then_inc(snm2, 1)
                p.raw("vector", nrm3)
                hcnt["snm"] += 1
                pav_free_at[q2] = hcnt["snm"]
            fin = dict(hcnt)
            for engn in ("tensor", "scalar", "vector"):
                def cl(e, f=fin):
                    e.wait_ge(spe2, f["spe"])
                    e.wait_ge(sac2, f["sac"])
                    e.wait_ge(snm2, f["snm"])
                    return e.nop()
                p.on(engn, cl)
            p.bar()
            for st in range(4):
                for half in range(2):
                    for c in range(NCH):
                        p.te(lambda e, c=c, st=st, half=half: e.matmul(
                            pscG[:, 0:512], y_fm[:, c * 512 + st * P: c * 512 + st * P + P],
                            wcp[:, c * D + half * 512: c * D + half * 512 + 512],
                            start=(c == 0), stop=(c == NCH - 1)))
                    p.bar()
                    p.ve(lambda e, half=half: e.tensor_copy(s_f32[:, half * 512:(half + 1) * 512], pscG[:, 0:512]))
                    p.bar()
                p.se_dma(lambda e, sl=sl, st=st: e.dma_start(
                    out=s_d[(sl * 4 + st) * P:(sl * 4 + st + 1) * P, :], in_=s_f32[:]))
                p.bar()

        p.build(block, sem)
    return nc


def build_b0c():
    nc = bass.Bass("TRN2", target_bir_lowering=False)
    dd = lambda n, s_, dt=F32, k="ExternalInput": nc.dram_tensor(n, s_, dt, kind=k).ap()
    x_d = dd("x", [TPC, D])
    s_in_d = dd("s", [TPC, D])
    xnT_d = dd("xnT", [D, TPC], BF16)
    win_d = dd("W_in", [D, D])
    binT_d = dd("b_inT", [P, NCH])
    wg_d = dd("W_gate", [D, 8])
    bg_d = dd("b_gate_bc", [P, 8])
    w1_d = dd("W1", [8, D, 4096])
    b1T_d = dd("b1T", [8, P, 32])
    w2_d = dd("W2", [8, 4096, D])
    b2_d = dd("b2", [8, D])
    wout_d = dd("W_out", [D, D])
    bout_d = dd("b_out", [1, D])
    ident_d = dd("ident", [P, P], BF16)
    identf_d = dd("identf", [P, P])
    ones_d = dd("ones_row", [1, P], BF16)
    xo_d = dd("xo", [TPC, D], k="ExternalOutput")

    GT = 1024
    NG = TPC // GT
    NT_G = GT // P

    p = Prog()
    with contextlib.ExitStack() as ctx:
        sb = lambda n, s_, dt=F32: ctx.enter_context(nc.sbuf_tensor(n, s_, dt))
        psm = lambda n, s_, dt=F32: ctx.enter_context(nc.psum_tensor(n, s_, dt))
        h_fm = [sb(f"hfm{c}", [P, TPC], BF16) for c in range(NCH)]
        mid = [sb(f"mid{c}", [P, GT], BF16) for c in range(32)]
        w2h = [sb(f"w2h{c}", [P, 512], BF16) for c in range(16)]
        moe = sb("moe", [P, NT_G * D])
        moefm = [sb(f"moefm{c}", [P, GT], BF16) for c in range(NCH)]
        wout = sb("wout", [P, NCH * D], BF16)
        wgate = sb("wgate", [P, NCH * 8], BF16)
        bgate = sb("bgate", [P, 8])
        binT = sb("binT_s", [P, NCH])
        b1T = sb("b1T_s", [P, 32])
        b2row = sb("b2row", [1, 8 * D], BF16)
        identf = sb("identf_s", [P, P])
        boutrow = sb("boutrow", [1, D], BF16)
        ident = sb("ident_s", [P, P], BF16)
        ones_row = sb("ones_s", [1, P], BF16)
        xnbuf = [sb(f"xnbuf{i}", [P, 512], BF16) for i in range(2)]
        w1buf = [sb(f"w1buf{i}", [P, 512], BF16) for i in range(2)]
        logits = sb("logits", [P, NT * 8])
        routes = sb("routes", [P, NT * 8])
        m1 = sb("m1", [P, 1])
        m2 = sb("m2", [P, 1])
        g1 = sb("g1", [P, 1])
        g2 = sb("g2", [P, 1])
        hsc = sb("hsc", [P, 8])
        xtile = None
        ptile = psm("ptile", [P, 4 * 512])
        pmid = psm("pmid", [P, 4 * 512])
        sem = ctx.enter_context(nc.semaphore("sem"))
        block = ctx.enter_context(nc.Block())

        p.sy(lambda e: e.dma_start(out=ident[:], in_=ident_d[:]))
        p.sy(lambda e: e.dma_start(out=identf[:], in_=identf_d[:]))
        p.sy(lambda e: e.dma_start(out=ones_row[:], in_=ones_d[:]))
        p.sy(lambda e: e.dma_start(out=binT[:], in_=binT_d[:]))
        p.sy(lambda e: e.dma_start(out=bgate[:], in_=bg_d[:]))
        p.gp(lambda e: e.dma_start(out=b2row[:], in_=b2_d.rearrange("e d -> (e d)").rearrange("(u f) -> u f", u=1)))
        p.gp(lambda e: e.dma_start(out=boutrow[:], in_=bout_d[:]))
        emit_cast_load(p, wout, wout_d, D, D)
        emit_cast_load(p, wgate, wg_d, D, 8)
        p.bar()

        for dc in range(NCH):
            for hlf in range(2):
                p.gp(lambda e, dc=dc, hlf=hlf: e.dma_start(
                    out=w2h[2 * dc + hlf][:], in_=win_d[dc * P:(dc + 1) * P, hlf * 512:(hlf + 1) * 512]))
        p.bar()
        def win_sl(dc, oc):
            return w2h[2 * dc + (oc // 4)][:, (oc % 4) * P:(oc % 4 + 1) * P]
        for tt in range(4):
            for dc in range(NCH):
                p.sy(lambda e, tt=tt, dc=dc: e.dma_start(
                    out=xnbuf[dc % 2][:], in_=xnT_d[dc * P:(dc + 1) * P, tt * 512:(tt + 1) * 512]))
                p.bar()
                for oc in range(NCH):
                    pb = ptile[:, (oc % 4) * 512:(oc % 4 + 1) * 512] if oc < 4 else pmid[:, (oc - 4) * 512:(oc - 3) * 512]
                    p.te(lambda e, pb=pb, dc=dc, oc=oc: e.matmul(
                        pb, win_sl(dc, oc), xnbuf[dc % 2][:],
                        start=(dc == 0), stop=(dc == NCH - 1)))
                p.bar()
            for oc in range(NCH):
                pb = ptile[:, (oc % 4) * 512:(oc % 4 + 1) * 512] if oc < 4 else pmid[:, (oc - 4) * 512:(oc - 3) * 512]
                p.ve(lambda e, pb=pb, oc=oc, tt=tt: e.tensor_scalar_add(
                    h_fm[oc][:, tt * 512:(tt + 1) * 512], pb, binT[:, oc:oc + 1]))
            p.bar()

        for t in range(NT):
            for dc in range(NCH):
                p.te(lambda e, t=t, dc=dc: e.matmul(
                    ptile[:, 0:8], h_fm[dc][:, t * P:(t + 1) * P], wgate[:, dc * 8:(dc + 1) * 8],
                    start=(dc == 0), stop=(dc == NCH - 1)))
            p.bar()
            lg = logits[:, t * 8:(t + 1) * 8]
            p.ve(lambda e, lg=lg: e.tensor_tensor(lg, ptile[:, 0:8], bgate[:], ALU.add))
            p.bar()
            p.ve(lambda e, lg=lg: e.reduce_max(m1[:], lg, AX.X))
            p.bar()
            p.ve(lambda e, lg=lg: e.tensor_single_scalar(hsc[:], lg, m1[:], ALU.is_equal))
            p.bar()
            p.ve(lambda e, lg=lg: e.scalar_tensor_tensor(hsc[:], hsc[:], -1e30, lg, ALU.mult, ALU.add))
            p.bar()
            p.ve(lambda e: e.reduce_max(m2[:], hsc[:], AX.X))
            p.bar()
            p.ve(lambda e: e.tensor_tensor(g1[:], m1[:], m2[:], ALU.subtract))
            p.bar()
            p.se(lambda e: e.activation(g1[:], g1[:], AF.Sigmoid))
            p.bar()
            p.ve(lambda e: e.tensor_scalar(g2[:], g1[:], -1.0, 1.0, ALU.mult, ALU.add))
            p.bar()
            rt = routes[:, t * 8:(t + 1) * 8]
            p.ve(lambda e, lg=lg, rt=rt: e.tensor_single_scalar(rt, lg, m1[:], ALU.is_equal))
            p.bar()
            p.ve(lambda e, rt=rt: e.tensor_scalar(rt, rt, g1[:], None, ALU.mult))
            p.bar()
            p.ve(lambda e: e.tensor_single_scalar(hsc[:], hsc[:], m2[:], ALU.is_equal))
            p.bar()
            p.ve(lambda e, rt=rt: e.scalar_tensor_tensor(hsc[:], hsc[:], g2[:], rt, ALU.mult, ALU.add))
            p.bar()
            p.ve(lambda e, rt=rt: e.tensor_copy(rt, hsc[:]))
            p.bar()

        for g in range(NG):
            p.ve(lambda e: e.memset(moe[:], 0.0))
            p.bar()
            for ex in range(8):
                p.sy(lambda e, ex=ex: e.dma_start(out=b1T[:], in_=b1T_d[ex, :, :]))
                p.bar()
                for hcg in range(8):
                    for dc in range(NCH):
                        p.gp(lambda e, ex=ex, hcg=hcg, dc=dc: e.dma_start(
                            out=w1buf[dc % 2][:], in_=w1_d[ex, dc * P:(dc + 1) * P, hcg * 512:(hcg + 1) * 512]))
                        p.bar()
                        for hc4 in range(4):
                            for t5 in range(GT // 512):
                                pb = (pmid if hc4 >= 2 else ptile)[:, ((hc4 % 2) * 2 + t5) * 512: ((hc4 % 2) * 2 + t5 + 1) * 512]
                                p.te(lambda e, pb=pb, dc=dc, hc4=hc4, t5=t5, g=g: e.matmul(
                                    pb, w1buf[dc % 2][:, hc4 * P:(hc4 + 1) * P],
                                    h_fm[dc][:, g * GT + t5 * 512: g * GT + (t5 + 1) * 512],
                                    start=(dc == 0), stop=(dc == NCH - 1)))
                        p.bar()
                    for hc4 in range(4):
                        hc = hcg * 4 + hc4
                        for t5 in range(GT // 512):
                            pb = (pmid if hc4 >= 2 else ptile)[:, ((hc4 % 2) * 2 + t5) * 512: ((hc4 % 2) * 2 + t5 + 1) * 512]
                            p.se(lambda e, pb=pb, hc=hc, t5=t5: e.activation(
                                mid[hc][:, t5 * 512:(t5 + 1) * 512], pb, AF.Relu, bias=b1T[:, hc:hc + 1]))
                    p.bar()
                def pb_of(t):
                    return (ptile if t < 4 else pmid)[:, (t % 4) * 512:(t % 4 + 1) * 512]
                for half in range(2):
                    for hg2 in range(2):
                        for hc16 in range(16):
                            p.gp(lambda e, ex=ex, hg2=hg2, hc16=hc16, half=half: e.dma_start(
                                out=w2h[hc16][:], in_=w2_d[ex, (hg2 * 16 + hc16) * P:(hg2 * 16 + hc16 + 1) * P,
                                                           half * 512:(half + 1) * 512]))
                        p.bar()
                        for t in range(NT_G):
                            for hc16 in range(16):
                                hc = hg2 * 16 + hc16
                                p.te(lambda e, t=t, hc=hc, hc16=hc16: e.matmul(
                                    pb_of(t), mid[hc][:, t * P:(t + 1) * P], w2h[hc16][:],
                                    start=(hc == 0), stop=False))
                        p.bar()
                    for t in range(NT_G):
                        p.te(lambda e, t=t, ex=ex, half=half: e.matmul(
                            pb_of(t), ones_row[:], b2row[0:1, ex * D + half * 512: ex * D + (half + 1) * 512],
                            start=False, stop=True))
                    p.bar()
                    for t in range(NT_G):
                        tg = g * NT_G + t
                        p.ve(lambda e, t=t, tg=tg, half=half, ex=ex: e.scalar_tensor_tensor(
                            moe[:, t * D + half * 512: t * D + (half + 1) * 512], pb_of(t),
                            routes[:, tg * 8 + ex: tg * 8 + ex + 1],
                            moe[:, t * D + half * 512: t * D + (half + 1) * 512],
                            ALU.mult, ALU.add))
                    p.bar()
            for t in range(NT_G):
                for c in range(NCH):
                    p.te(lambda e, t=t, c=c: e.transpose(pmid[:, 1536:1664], moe[:, t * D + c * P: t * D + (c + 1) * P], identf[:]))
                    p.bar()
                    p.ve(lambda e, t=t, c=c: e.tensor_copy(moefm[c][:, t * P:(t + 1) * P], pmid[:, 1536:1664]))
                    p.bar()
            xtile = moe[:, 0:D]
            stile = moe[:, D:2 * D]
            gtile = moe[:, 2 * D:3 * D]
            for t in range(NT_G):
                tg = g * NT_G + t
                p.sy(lambda e, tg=tg: e.dma_start(out=xtile, in_=x_d[tg * P:(tg + 1) * P, :]))
                p.sy(lambda e, tg=tg: e.dma_start(out=stile, in_=s_in_d[tg * P:(tg + 1) * P, :]))
                for half in range(2):
                    pb = ptile[:, half * 512:(half + 1) * 512]
                    for c in range(NCH):
                        p.te(lambda e, pb=pb, c=c, t=t, half=half: e.matmul(
                            pb, moefm[c][:, t * P:(t + 1) * P], wout[:, c * D + half * 512: c * D + (half + 1) * 512],
                            start=(c == 0), stop=False))
                    p.te(lambda e, pb=pb, half=half: e.matmul(
                        pb, ones_row[:], boutrow[0:1, half * 512:(half + 1) * 512], start=False, stop=True))
                p.bar()
                for half in range(2):
                    pb = ptile[:, half * 512:(half + 1) * 512]
                    p.se(lambda e, pb=pb, half=half: e.activation(
                        gtile[:, half * 512:(half + 1) * 512], pb, AF.Silu))
                p.bar()
                p.ve(lambda e: e.tensor_tensor(gtile, stile, gtile, ALU.mult))
                p.bar()
                p.ve(lambda e: e.tensor_tensor(gtile, xtile, gtile, ALU.add))
                p.bar()
                p.se_dma(lambda e, tg=tg: e.dma_start(out=xo_d[tg * P:(tg + 1) * P, :], in_=gtile))
                p.bar()

        p.build(block, sem)
    return nc


def build_b1a():
    nc = bass.Bass("TRN2", target_bir_lowering=False)
    dd = lambda n, s_, dt=F32, k="ExternalInput": nc.dram_tensor(n, s_, dt, kind=k).ap()
    x_d = dd("x", [TPC, D])
    wqkvT_d = dd("WqkvT", [D, 3 * D])
    bqkv_d = dd("bqkv_bc", [P, 3 * D])
    wprojT_d = dd("WprojT", [D, D])
    bproj_d = dd("bproj_bc", [P, D])
    gwT_d = dd("gate_wT", [D, 8])
    ident_d = dd("ident", [P, P], BF16)
    s_d = dd("s", [TPC, D], k="ExternalOutput")
    xnT_d = dd("xnT", [D, TPC], BF16, k="ExternalOutput")
    lg_d = dd("logits", [TPC, 8], k="ExternalOutput")
    xnscr_d = dd("xnscr", [TPC, D], BF16, k="Internal")
    aoscr_d = dd("aoscr", [TPC, D], BF16, k="Internal")

    p = Prog()
    with contextlib.ExitStack() as ctx:
        sb = lambda n, s_, dt=F32: ctx.enter_context(nc.sbuf_tensor(n, s_, dt))
        psm = lambda n, s_, dt=F32: ctx.enter_context(nc.psum_tensor(n, s_, dt))
        xt = [sb(f"xt{i}", [P, D]) for i in range(2)]
        sq = sb("sq", [P, D])
        lncols = [[sb(f"ln{i}{n}", [P, 1]) for n in ("mc", "vc", "rc", "sr")] for i in range(2)]
        epscol = sb("epscol", [P, 1])
        mhalf = sb("mhalf", [P, 1])
        xn_bf2 = [sb(f"xnbf{i}", [P, D], BF16) for i in range(2)]
        xn_fm = [sb(f"xnfm{c}", [P, TPC], BF16) for c in range(NCH)]
        att_fm = [sb(f"attfm{c}", [P, TPC], BF16) for c in range(NCH)]
        wqkvT = sb("wqkvT", [P, NCH * 3 * D], BF16)
        bqkv = sb("bqkv", [P, 3 * D])
        wprojT = sb("wprojT", [P, NCH * D], BF16)
        bproj = sb("bproj", [P, D])
        gwT = sb("gwT", [P, NCH * 8], BF16)
        qkv0 = sb("qkv0", [P, 3 * D])
        qkv1 = sb("qkv1", [P, 3 * D])
        dots = sb("dots", [P, D])
        sc16 = sb("sc16", [P, 16])
        sg0 = sb("sg0", [P, 16])
        sg1 = sb("sg1", [P, 16])
        vd = sb("vd", [P, D])
        ao = sb("ao", [P, D], BF16)
        lgsb = sb("lgsb", [P, 8])
        pq = psm("pq", [P, 6 * 512])
        px = psm("px", [P, 512])
        sem = ctx.enter_context(nc.semaphore("sem"))
        block = ctx.enter_context(nc.Block())

        p.sy(lambda e: e.dma_start(out=bqkv[:], in_=bqkv_d[:]))
        p.sy(lambda e: e.dma_start(out=bproj[:], in_=bproj_d[:]))
        emit_cast_load(p, wqkvT, wqkvT_d, D, 3 * D)
        emit_cast_load(p, wprojT, wprojT_d, D, D)
        emit_cast_load(p, gwT, gwT_d, D, 8)
        p.sy(lambda e: e.dma_start(out=xt[0][:], in_=x_d[0:P, :]))
        p.ve(lambda e: e.memset(epscol[:], 1e-5))
        p.ve(lambda e: e.memset(mhalf[:], -0.5))
        p.bar()

        lnbufs = dict(sq=sq, cols=lncols, eps=epscol, mhalf=mhalf, xn_bf=xn_bf2)
        emit_ln_stream_to_fm(p, None, x_d, NT, xt, lnbufs, xn_fm, xnscr_d)
        for c in range(NCH):
            p.se_dma(lambda e, c=c: e.dma_start(out=xnT_d[c * P:(c + 1) * P, :], in_=xn_fm[c][:]))
        p.bar()

        for t in range(NT):
            for dc in range(NCH):
                p.te(lambda e, t=t, dc=dc: e.matmul(
                    px[:, 0:8], xn_fm[dc][:, t * P:(t + 1) * P], gwT[:, dc * 8:(dc + 1) * 8],
                    start=(dc == 0), stop=(dc == NCH - 1)))
            p.bar()
            p.ve(lambda e: e.tensor_copy(lgsb[:], px[:, 0:8]))
            p.bar()
            p.se_dma(lambda e, t=t: e.dma_start(out=lg_d[t * P:(t + 1) * P, :], in_=lgsb[:]))
            p.bar()

        for jt in range(NT // 2):
            for k, tile in enumerate((jt, jt + 8)):
                dst = qkv0 if k == 0 else qkv1
                for dc in range(NCH):
                    for o6 in range(6):
                        p.te(lambda e, dc=dc, o6=o6, tile=tile: e.matmul(
                            pq[:, o6 * 512:(o6 + 1) * 512],
                            xn_fm[dc][:, tile * P:(tile + 1) * P],
                            wqkvT[:, dc * 3 * D + o6 * 512: dc * 3 * D + (o6 + 1) * 512],
                            start=(dc == 0), stop=(dc == NCH - 1)))
                p.bar()
                for o6 in range(6):
                    p.ve(lambda e, o6=o6, dst=dst: e.tensor_tensor(
                        dst[:, o6 * 512:(o6 + 1) * 512], pq[:, o6 * 512:(o6 + 1) * 512],
                        bqkv[:, o6 * 512:(o6 + 1) * 512], ALU.add))
                p.bar()
            q0, k0, v0 = (qkv0[:, i * D:(i + 1) * D] for i in range(3))
            q1, k1, v1 = (qkv1[:, i * D:(i + 1) * D] for i in range(3))
            hview = lambda a: a.rearrange("p (h d) -> p h d", h=16)
            red = lambda dst, a, b: (
                p.ve(lambda e, a=a, b=b: e.tensor_tensor(dots[:], a, b, ALU.mult)),
                p.bar(),
                p.ve(lambda e, dst=dst: e.tensor_reduce(
                    dst.rearrange("p (h u) -> p h u", u=1), hview(dots[:]), AX.X, ALU.add)),
                p.bar())
            red(sc16[:], q0, k0)
            p.ve(lambda e: e.tensor_copy(sg0[:], sc16[:]))
            p.bar()
            red(sc16[:], q0, k1)
            p.ve(lambda e: e.tensor_tensor(sg0[:], sg0[:], sc16[:], ALU.subtract))
            p.bar()
            red(sc16[:], q1, k0)
            p.ve(lambda e: e.tensor_copy(sg1[:], sc16[:]))
            p.bar()
            red(sc16[:], q1, k1)
            p.ve(lambda e: e.tensor_tensor(sg1[:], sg1[:], sc16[:], ALU.subtract))
            p.bar()
            p.ve(lambda e: e.tensor_scalar_mul(sg0[:], sg0[:], 0.125))
            p.ve(lambda e: e.tensor_scalar_mul(sg1[:], sg1[:], 0.125))
            p.bar()
            p.se(lambda e: e.activation(sg0[:], sg0[:], AF.Sigmoid))
            p.bar()
            p.se(lambda e: e.activation(sg1[:], sg1[:], AF.Sigmoid))
            p.bar()
            p.ve(lambda e: e.tensor_tensor(vd[:], v0, v1, ALU.subtract))
            p.bar()
            for k, (sg, tile) in enumerate(((sg0, jt), (sg1, jt + 8))):
                sgb = sg[:].rearrange("p (h u) -> p h u", u=1).broadcast_to([P, 16, 64])
                p.ve(lambda e, sgb=sgb: e.tensor_tensor(hview(ao[:]), hview(vd[:]), sgb, ALU.mult))
                p.bar()
                p.ve(lambda e: e.tensor_tensor(ao[:], ao[:], v1, ALU.add))
                p.bar()
                p.se_dma(lambda e, tile=tile: e.dma_start(out=aoscr_d[tile * P:(tile + 1) * P, :], in_=ao[:]))
                p.bar()

        for c in range(NCH):
            p.sy(lambda e, c=c: e.dma_start_transpose(att_fm[c][:], aoscr_d[:, c * P:(c + 1) * P]))
        p.bar()
        for t in range(NT):
            for half in range(2):
                for dc in range(NCH):
                    p.te(lambda e, t=t, half=half, dc=dc: e.matmul(
                        pq[:, half * 512:(half + 1) * 512],
                        att_fm[dc][:, t * P:(t + 1) * P],
                        wprojT[:, dc * D + half * 512: dc * D + (half + 1) * 512],
                        start=(dc == 0), stop=(dc == NCH - 1)))
            p.bar()
            for half in range(2):
                p.ve(lambda e, half=half: e.tensor_tensor(
                    qkv0[:, half * 512:(half + 1) * 512], pq[:, half * 512:(half + 1) * 512],
                    bproj[:, half * 512:(half + 1) * 512], ALU.add))
            p.bar()
            p.se_dma(lambda e, t=t: e.dma_start(out=s_d[t * P:(t + 1) * P, :], in_=qkv0[:, 0:D]))
            p.bar()

        p.build(block, sem)
    return nc


def build_b1rt():
    nc = bass.Bass("TRN2", target_bir_lowering=False)
    dd = lambda n, s_, dt=F32, k="ExternalInput": nc.dram_tensor(n, s_, dt, kind=k).ap()
    lg_d = dd("logits_re", [P, P * 8])
    tri_i_d = dd("tri_incl", [P, P], BF16)
    tri_e_d = dd("tri_excl", [P, P], BF16)
    ones_d = dd("allones", [P, P], BF16)
    coef_d = dd("coef", [P, P * 8], k="ExternalOutput")
    scr1_d = dd("scr1", [1, P * 8], k="Internal")
    scr2_d = dd("scr2", [2, P, 8], k="Internal")

    CAP = 4096.0
    p = Prog()
    with contextlib.ExitStack() as ctx:
        sb = lambda n, s_, dt=F32: ctx.enter_context(nc.sbuf_tensor(n, s_, dt))
        psm = lambda n, s_, dt=F32: ctx.enter_context(nc.psum_tensor(n, s_, dt))
        lg = sb("lg", [P, P * 8])
        tri_i = sb("tri_i", [P, P], BF16)
        tri_e = sb("tri_e", [P, P], BF16)
        aones = sb("aones", [P, P], BF16)
        m1 = sb("m1", [P, P])
        m2 = sb("m2", [P, P])
        h1 = sb("h1", [P, P * 8])
        h2 = sb("h2", [P, P * 8])
        h1b = sb("h1b", [P, P * 8], BF16)
        h2b = sb("h2b", [P, P * 8], BF16)
        msk = sb("msk", [P, P * 8])
        C1 = sb("C1", [P, P * 8])
        C2 = sb("C2", [P, P * 8])
        S12 = sb("S12", [P, 16], BF16)
        Poff = sb("Poff", [P, 16])
        pbc = sb("pbc", [P, P * 8])
        V = sb("V", [P, P])
        a1 = sb("a1", [P, P])
        a2 = sb("a2", [P, P])
        g1 = sb("g1", [P, P])
        coef = sb("coef_s", [P, P * 8])
        pp = psm("pp", [P, 4 * 512])
        sem = ctx.enter_context(nc.semaphore("sem"))
        block = ctx.enter_context(nc.Block())

        p.sy(lambda e: e.dma_start(out=lg[:], in_=lg_d[:]))
        p.sy(lambda e: e.dma_start(out=tri_i[:], in_=tri_i_d[:]))
        p.sy(lambda e: e.dma_start(out=tri_e[:], in_=tri_e_d[:]))
        p.sy(lambda e: e.dma_start(out=aones[:], in_=ones_d[:]))
        p.bar()
        ev = lambda a: a.rearrange("p (t e) -> p t e", e=8)
        evb = lambda a: a.rearrange("p (t u) -> p t u", u=1)
        p.ve(lambda e: e.tensor_reduce(evb(m1[:]), ev(lg[:]), AX.X, ALU.max))
        p.bar()
        p.ve(lambda e: e.tensor_tensor(ev(h1[:]), ev(lg[:]), evb(m1[:]).broadcast_to([P, P, 8]), ALU.is_equal))
        p.bar()
        p.ve(lambda e: e.scalar_tensor_tensor(msk[:], h1[:], -1e30, lg[:], ALU.mult, ALU.add))
        p.bar()
        p.ve(lambda e: e.tensor_reduce(evb(m2[:]), ev(msk[:]), AX.X, ALU.max))
        p.bar()
        p.ve(lambda e: e.tensor_tensor(ev(h2[:]), ev(msk[:]), evb(m2[:]).broadcast_to([P, P, 8]), ALU.is_equal))
        p.bar()
        p.ve(lambda e: e.tensor_copy(h1b[:], h1[:]))
        p.ve(lambda e: e.tensor_copy(h2b[:], h2[:]))
        p.bar()
        for half in range(2):
            p.te(lambda e, half=half: e.matmul(pp[:, half * 512:(half + 1) * 512], tri_i[:],
                                               h1b[:, half * 512:(half + 1) * 512], start=True, stop=True))
            p.te(lambda e, half=half: e.matmul(pp[:, (2 + half) * 512:(3 + half) * 512], tri_i[:],
                                               h2b[:, half * 512:(half + 1) * 512], start=True, stop=True))
        p.bar()
        for half in range(2):
            p.ve(lambda e, half=half: e.tensor_copy(C1[:, half * 512:(half + 1) * 512], pp[:, half * 512:(half + 1) * 512]))
            p.ve(lambda e, half=half: e.tensor_copy(C2[:, half * 512:(half + 1) * 512], pp[:, (2 + half) * 512:(3 + half) * 512]))
        p.bar()
        p.se_dma(lambda e: e.dma_start(out=scr1_d[0:1, :], in_=C1[127:128, :]))
        p.bar()
        p.gp(lambda e: e.dma_start(out=S12[:, 0:8], in_=scr1_d.rearrange("u (t e) -> (u t) e", e=8)))
        p.bar()
        p.se_dma(lambda e: e.dma_start(out=scr1_d[0:1, :], in_=C2[127:128, :]))
        p.bar()
        p.gp(lambda e: e.dma_start(out=S12[:, 8:16], in_=scr1_d.rearrange("u (t e) -> (u t) e", e=8)))
        p.bar()
        p.te(lambda e: e.matmul(pp[:, 0:8], tri_e[:], S12[:, 0:8], start=True, stop=True))
        p.te(lambda e: e.matmul(pp[:, 512:520], tri_e[:], S12[:, 8:16], start=True, stop=False))
        p.te(lambda e: e.matmul(pp[:, 512:520], aones[:], S12[:, 0:8], start=False, stop=True))
        p.bar()
        p.ve(lambda e: e.tensor_copy(Poff[:, 0:8], pp[:, 0:8]))
        p.ve(lambda e: e.tensor_copy(Poff[:, 8:16], pp[:, 512:520]))
        p.bar()
        p.se_dma(lambda e: e.dma_start(out=scr2_d[0], in_=Poff[:, 0:8]))
        p.se_dma(lambda e: e.dma_start(out=scr2_d[1], in_=Poff[:, 8:16]))
        p.bar()
        for j, (Cj, hj, aj) in enumerate(((C1, h1, a1), (C2, h2, a2))):
            p.sy(lambda e, j=j: e.dma_start(
                out=pbc[:], in_=scr2_d[j].rearrange("t e -> (t e)")
                .rearrange("(u f) -> u f", u=1).broadcast_to([P, P * 8])))
            p.bar()
            p.ve(lambda e, Cj=Cj: e.tensor_tensor(pbc[:], pbc[:], Cj[:], ALU.add))
            p.bar()
            p.ve(lambda e, hj=hj: e.tensor_tensor(pbc[:], pbc[:], hj[:], ALU.mult))
            p.bar()
            p.ve(lambda e: e.tensor_reduce(evb(V[:]), ev(pbc[:]), AX.X, ALU.add))
            p.bar()
            p.ve(lambda e, aj=aj: e.tensor_single_scalar(aj[:], V[:], CAP, ALU.is_le))
            p.bar()
        p.ve(lambda e: e.tensor_tensor(g1[:], m1[:], m2[:], ALU.subtract))
        p.bar()
        p.se(lambda e: e.activation(g1[:], g1[:], AF.Sigmoid))
        p.bar()
        p.ve(lambda e: e.tensor_tensor(a1[:], a1[:], g1[:], ALU.mult))
        p.bar()
        p.ve(lambda e: e.tensor_scalar(g1[:], g1[:], -1.0, 1.0, ALU.mult, ALU.add))
        p.bar()
        p.ve(lambda e: e.tensor_tensor(a2[:], a2[:], g1[:], ALU.mult))
        p.bar()
        p.ve(lambda e: e.tensor_tensor(ev(coef[:]), ev(h1[:]), evb(a1[:]).broadcast_to([P, P, 8]), ALU.mult))
        p.bar()
        p.ve(lambda e: e.tensor_tensor(ev(h2[:]), ev(h2[:]), evb(a2[:]).broadcast_to([P, P, 8]), ALU.mult))
        p.bar()
        p.ve(lambda e: e.tensor_tensor(coef[:], coef[:], h2[:], ALU.add))
        p.bar()
        p.se_dma(lambda e: e.dma_start(out=coef_d[:], in_=coef[:]))
        p.bar()
        p.build(block, sem)
    return nc


def build_b1c():
    nc = bass.Bass("TRN2", target_bir_lowering=False)
    dd = lambda n, s_, dt=F32, k="ExternalInput": nc.dram_tensor(n, s_, dt, kind=k).ap()
    x_d = dd("x", [TPC, D])
    s_in_d = dd("s", [TPC, D])
    xnT_d = dd("xnT", [D, TPC], BF16)
    rt_d = dd("routes", [TPC, 8])
    w1_d = dd("fc1_wT", [8, D, 4096])
    b1T_d = dd("fc1_bT", [8, P, 32])
    w2_d = dd("fc2_w", [8, 4096, D])
    b2_d = dd("fc2_b", [8, D])
    ones_d = dd("ones_row", [1, P], BF16)
    xo_d = dd("xo", [TPC, D], k="ExternalOutput")

    GT = 1024
    NG = TPC // GT
    NT_G = GT // P

    p = Prog()
    with contextlib.ExitStack() as ctx:
        sb = lambda n, s_, dt=F32: ctx.enter_context(nc.sbuf_tensor(n, s_, dt))
        psm = lambda n, s_, dt=F32: ctx.enter_context(nc.psum_tensor(n, s_, dt))
        xn_fm = [sb(f"xnfm{c}", [P, TPC], BF16) for c in range(NCH)]
        mid = [sb(f"mid{c}", [P, GT], BF16) for c in range(32)]
        w2h = [sb(f"w2h{c}", [P, 512], BF16) for c in range(16)]
        moe = sb("moe", [P, NT_G * D])
        routes = sb("routes_s", [P, NT * 8])
        b1T = sb("b1T_s", [P, 32])
        b2row = sb("b2row", [1, 8 * D], BF16)
        ones_row = sb("ones_s", [1, P], BF16)
        w1buf = [sb(f"w1buf{i}", [P, 512], BF16) for i in range(2)]
        xt_ = sb("xt_", [P, D])
        st_ = sb("st_", [P, D])
        gt_ = sb("gt_", [P, D])
        ptile = psm("ptile", [P, 4 * 512])
        pmid = psm("pmid", [P, 4 * 512])
        sem = ctx.enter_context(nc.semaphore("sem"))
        block = ctx.enter_context(nc.Block())

        for c in range(NCH):
            p.sy(lambda e, c=c: e.dma_start(out=xn_fm[c][:], in_=xnT_d[c * P:(c + 1) * P, :]))
        for t in range(NT):
            p.sy(lambda e, t=t: e.dma_start(out=routes[:, t * 8:(t + 1) * 8], in_=rt_d[t * P:(t + 1) * P, :]))
        p.gp(lambda e: e.dma_start(out=b2row[:], in_=b2_d.rearrange("e d -> (e d)").rearrange("(u f) -> u f", u=1)))
        p.sy(lambda e: e.dma_start(out=ones_row[:], in_=ones_d[:]))
        p.bar()

        def pb_of(t):
            return (ptile if t < 4 else pmid)[:, (t % 4) * 512:(t % 4 + 1) * 512]

        for g in range(NG):
            p.ve(lambda e: e.memset(moe[:], 0.0))
            p.bar()
            for ex in range(8):
                p.sy(lambda e, ex=ex: e.dma_start(out=b1T[:], in_=b1T_d[ex, :, :]))
                p.bar()
                for hcg in range(8):
                    for dc in range(NCH):
                        p.gp(lambda e, ex=ex, hcg=hcg, dc=dc: e.dma_start(
                            out=w1buf[dc % 2][:], in_=w1_d[ex, dc * P:(dc + 1) * P, hcg * 512:(hcg + 1) * 512]))
                        p.bar()
                        for hc4 in range(4):
                            for t5 in range(GT // 512):
                                pb = (pmid if hc4 >= 2 else ptile)[:, ((hc4 % 2) * 2 + t5) * 512: ((hc4 % 2) * 2 + t5 + 1) * 512]
                                p.te(lambda e, pb=pb, dc=dc, hc4=hc4, t5=t5, g=g: e.matmul(
                                    pb, w1buf[dc % 2][:, hc4 * P:(hc4 + 1) * P],
                                    xn_fm[dc][:, g * GT + t5 * 512: g * GT + (t5 + 1) * 512],
                                    start=(dc == 0), stop=(dc == NCH - 1)))
                        p.bar()
                    for hc4 in range(4):
                        hc = hcg * 4 + hc4
                        for t5 in range(GT // 512):
                            pb = (pmid if hc4 >= 2 else ptile)[:, ((hc4 % 2) * 2 + t5) * 512: ((hc4 % 2) * 2 + t5 + 1) * 512]
                            p.se(lambda e, pb=pb, hc=hc, t5=t5: e.activation(
                                mid[hc][:, t5 * 512:(t5 + 1) * 512], pb, AF.Relu, bias=b1T[:, hc:hc + 1]))
                    p.bar()
                for half in range(2):
                    for hg2 in range(2):
                        for hc16 in range(16):
                            p.gp(lambda e, ex=ex, hg2=hg2, hc16=hc16, half=half: e.dma_start(
                                out=w2h[hc16][:], in_=w2_d[ex, (hg2 * 16 + hc16) * P:(hg2 * 16 + hc16 + 1) * P,
                                                           half * 512:(half + 1) * 512]))
                        p.bar()
                        for t in range(NT_G):
                            for hc16 in range(16):
                                hc = hg2 * 16 + hc16
                                p.te(lambda e, t=t, hc=hc, hc16=hc16: e.matmul(
                                    pb_of(t), mid[hc][:, t * P:(t + 1) * P], w2h[hc16][:],
                                    start=(hc == 0), stop=False))
                        p.bar()
                    for t in range(NT_G):
                        p.te(lambda e, t=t, ex=ex, half=half: e.matmul(
                            pb_of(t), ones_row[:], b2row[0:1, ex * D + half * 512: ex * D + (half + 1) * 512],
                            start=False, stop=True))
                    p.bar()
                    for t in range(NT_G):
                        tg = g * NT_G + t
                        p.ve(lambda e, t=t, tg=tg, half=half, ex=ex: e.scalar_tensor_tensor(
                            moe[:, t * D + half * 512: t * D + (half + 1) * 512], pb_of(t),
                            routes[:, tg * 8 + ex: tg * 8 + ex + 1],
                            moe[:, t * D + half * 512: t * D + (half + 1) * 512],
                            ALU.mult, ALU.add))
                    p.bar()
            for t in range(NT_G):
                tg = g * NT_G + t
                p.sy(lambda e, tg=tg: e.dma_start(out=xt_[:], in_=x_d[tg * P:(tg + 1) * P, :]))
                p.sy(lambda e, tg=tg: e.dma_start(out=st_[:], in_=s_in_d[tg * P:(tg + 1) * P, :]))
                p.bar()
                p.se(lambda e, t=t: e.activation(gt_[:], moe[:, t * D:(t + 1) * D], AF.Silu))
                p.bar()
                p.ve(lambda e: e.tensor_tensor(gt_[:], st_[:], gt_[:], ALU.mult))
                p.bar()
                p.ve(lambda e: e.tensor_tensor(gt_[:], xt_[:], gt_[:], ALU.add))
                p.bar()
                p.se_dma(lambda e, tg=tg: e.dma_start(out=xo_d[tg * P:(tg + 1) * P, :], in_=gt_[:]))
                p.bar()

        p.build(block, sem)
    return nc


def build_b2a():
    nc = bass.Bass("TRN2", target_bir_lowering=False)
    dd = lambda n, s_, dt=F32, k="ExternalInput": nc.dram_tensor(n, s_, dt, kind=k).ap()
    x_d = dd("x", [TPC, D])
    wqkv_d = dd("qkv_w", [D, 3 * D])
    bqkvT_d = dd("qkv_bT", [P, 24])
    wrq_d = dd("rq_w", [D, 256])
    brqT_d = dd("rq_bT", [P, 2])
    ident_d = dd("ident", [P, P], BF16)
    xn_d = dd("xn", [TPC, D], BF16, k="ExternalOutput")
    qkvT_d = dd("qkvT", [3 * D, TPC], BF16, k="ExternalOutput")
    rqT_d = dd("rqT", [256, TPC], BF16, k="ExternalOutput")
    pooledT_d = dd("pooledT", [D, 16], k="ExternalOutput")
    xnscr_d = dd("xnscr", [TPC, D], BF16, k="Internal")

    p = Prog()
    with contextlib.ExitStack() as ctx:
        sb = lambda n, s_, dt=F32: ctx.enter_context(nc.sbuf_tensor(n, s_, dt))
        psm = lambda n, s_, dt=F32: ctx.enter_context(nc.psum_tensor(n, s_, dt))
        xt = [sb(f"xt{i}", [P, D]) for i in range(2)]
        sq = sb("sq", [P, D])
        lncols = [[sb(f"ln{i}{n}", [P, 1]) for n in ("mc", "vc", "rc", "sr")] for i in range(2)]
        epscol = sb("epscol", [P, 1])
        mhalf = sb("mhalf", [P, 1])
        xn_bf2 = [sb(f"xnbf{i}", [P, D], BF16) for i in range(2)]
        xn_fm = [sb(f"xnfm{c}", [P, TPC], BF16) for c in range(NCH)]
        wqkv = sb("wqkv", [P, NCH * 3 * D], BF16)
        wrq = sb("wrq", [P, NCH * 256], BF16)
        bqkvT = sb("bqkvT", [P, 24])
        brqT = sb("brqT", [P, 2])
        obuf = sb("obuf", [P, TPC], BF16)
        pool_c = sb("pool_c", [P, 16])
        pk = psm("pk", [P, 4 * 512])
        sem = ctx.enter_context(nc.semaphore("sem"))
        block = ctx.enter_context(nc.Block())

        p.sy(lambda e: e.dma_start(out=bqkvT[:], in_=bqkvT_d[:]))
        p.sy(lambda e: e.dma_start(out=brqT[:], in_=brqT_d[:]))
        emit_cast_load(p, wqkv, wqkv_d, D, 3 * D)
        emit_cast_load(p, wrq, wrq_d, D, 256)
        p.sy(lambda e: e.dma_start(out=xt[0][:], in_=x_d[0:P, :]))
        p.ve(lambda e: e.memset(epscol[:], 1e-5))
        p.ve(lambda e: e.memset(mhalf[:], -0.5))
        p.bar()

        lnbufs = dict(sq=sq, cols=lncols, eps=epscol, mhalf=mhalf, xn_bf=xn_bf2)
        emit_ln_stream_to_fm(p, None, x_d, NT, xt, lnbufs, xn_fm, xnscr_d, xn_tm_out=xn_d)

        for dc in range(NCH):
            p.ve(lambda e, dc=dc: e.tensor_reduce(
                pool_c[:].rearrange("p (t u) -> p t u", u=1),
                xn_fm[dc][:].rearrange("p (t c) -> p t c", t=16), AX.X, ALU.add))
            p.bar()
            p.ve(lambda e: e.tensor_scalar_mul(pool_c[:], pool_c[:], 1.0 / P))
            p.bar()
            p.se_dma(lambda e, dc=dc: e.dma_start(out=pooledT_d[dc * P:(dc + 1) * P, :], in_=pool_c[:]))
            p.bar()

        def proj_out(nout_ch, w_sb, w_stride, bT, out_d):
            for oc in range(nout_ch):
                for dc in range(NCH):
                    for tt in range(4):
                        p.te(lambda e, oc=oc, dc=dc, tt=tt: e.matmul(
                            pk[:, tt * 512:(tt + 1) * 512],
                            w_sb[:, dc * w_stride + oc * P: dc * w_stride + oc * P + P],
                            xn_fm[dc][:, tt * 512:(tt + 1) * 512],
                            start=(dc == 0), stop=(dc == NCH - 1)))
                p.bar()
                for tt in range(4):
                    p.ve(lambda e, oc=oc, tt=tt: e.tensor_scalar_add(
                        obuf[:, tt * 512:(tt + 1) * 512], pk[:, tt * 512:(tt + 1) * 512],
                        bT[:, oc:oc + 1]))
                p.bar()
                p.se_dma(lambda e, oc=oc: e.dma_start(out=out_d[oc * P:(oc + 1) * P, :], in_=obuf[:]))
                p.bar()
        proj_out(24, wqkv, 3 * D, bqkvT, qkvT_d)
        proj_out(2, wrq, 256, brqT, rqT_d)

        p.build(block, sem)
    return nc


def build_b2b():
    nc = bass.Bass("TRN2", target_bir_lowering=False)
    dd = lambda n, s_, dt=F32, k="ExternalInput": nc.dram_tensor(n, s_, dt, kind=k).ap()
    qT_d = dd("qT", [D, TPC], BF16)
    kTe_d = dd("kTe", [D, TPC + P], BF16)
    vauge_d = dd("vauge", [16, 17, P, 65], BF16)
    masks_d = dd("masks", [8, P, 768], BF16)
    rqT_d = dd("rqT", [256, TPC], BF16)
    pooledT_d = dd("pooledT_b", [D, 64])
    lmask_d = dd("lmask", [16, P, 64], BF16)
    rkvw_d = dd("rkv_w", [D, 512])
    rkvbT_d = dd("rkv_bT", [P, 4])
    routw_d = dd("rout_w", [256, D])
    routbT_d = dd("rout_bT", [P, 8])
    outw_d = dd("out_w", [D, D])
    outb_d = dd("out_b_row", [1, D])
    glr_d = dd("glr2", [P, 2])
    ones_d = dd("ones_row", [1, P], BF16)
    ident_d = dd("ident", [P, P], BF16)
    s_d = dd("s", [TPC, D], k="ExternalOutput")

    p = Prog()
    with contextlib.ExitStack() as ctx:
        sb = lambda n, s_, dt=F32: ctx.enter_context(nc.sbuf_tensor(n, s_, dt))
        psm = lambda n, s_, dt=F32: ctx.enter_context(nc.psum_tensor(n, s_, dt))
        q_sb = [sb(f"q{c}", [P, TPC], BF16) for c in range(NCH)]
        kTe = [sb(f"k{c}", [P, TPC + P], BF16) for c in range(NCH)]
        vau = sb("vau", [P, 8 * 17 * 65], BF16)
        masks = sb("masks_s", [P, 8 * 768], BF16)
        rq_sb = [sb(f"rq{c}", [P, TPC], BF16) for c in range(2)]
        latfm = [sb(f"lat{c}", [P, TPC], BF16) for c in range(2)]
        rem = [sb(f"rem{c}", [P, TPC], BF16) for c in range(NCH)]
        yh = [sb(f"yh{c}", [P, TPC], BF16) for c in range(4)]
        lmask = sb("lmask_s", [P, 16 * 64], BF16)
        pooledT = sb("pooledT_s", [P, NCH * 64], BF16)
        rkvw = sb("rkvw", [P, NCH * 512], BF16)
        rkvbT = sb("rkvbT_s", [P, 4])
        rkv_fm = sb("rkv_fm", [P, 4 * 64], BF16)
        rvT = sb("rvT", [64, 4 * 64], BF16)
        routw = sb("routw", [P, 2 * D], BF16)
        routbT = sb("routbT_s", [P, 8])
        outw = sb("outw", [P, NCH * D], BF16)
        outb = sb("outb_row", [1, D], BF16)
        glr = sb("glr_s", [P, 2], BF16)
        ones_row = sb("ones_s", [1, P], BF16)
        ident = sb("ident_s", [P, P], BF16)
        mcsh = sb("mcsh", [P, 1])
        zrow = sb("zrow", [P, 64])
        zmx = sb("zmx", [P, 1])
        zsum = sb("zsum", [P, 1])
        Pl = sb("Pl", [P, 64], BF16)
        PnT = sb("PnT", [P, P], BF16)
        Pw = sb("Pw", [P, 768], BF16)
        srow = sb("srow", [1, 512])
        rbf = sb("rbf", [1, 512], BF16)
        rb_sb = sb("rb_sb", [P, 512])
        gd = sb("gd", [P, 512])
        grow = srow
        grbf = rbf
        s_f32 = sb("s_f32", [P, D])
        pw = psm("pw", [P, 2 * 512])
        pav = psm("pav", [65, 512])
        pl = psm("pl", [P, 512])
        ptb = psm("ptb", [P, P], BF16)
        pz = psm("pz", [P, 2 * 512])
        sem = ctx.enter_context(nc.semaphore("sem"))
        block = ctx.enter_context(nc.Block())

        for c in range(NCH):
            p.sy(lambda e, c=c: e.dma_start(out=q_sb[c][:], in_=qT_d[c * P:(c + 1) * P, :]))
            p.sy(lambda e, c=c: e.dma_start(out=kTe[c][:], in_=kTe_d[c * P:(c + 1) * P, :]))
        for c in range(2):
            p.sy(lambda e, c=c: e.dma_start(out=rq_sb[c][:], in_=rqT_d[c * P:(c + 1) * P, :]))
        p.sy(lambda e: e.dma_start(out=masks[:].rearrange("p (b c) -> p b c", b=8),
                                   in_=masks_d.rearrange("b p c -> p b c")))
        p.sy(lambda e: e.dma_start(out=lmask[:].rearrange("p (t c) -> p t c", t=16),
                                   in_=lmask_d.rearrange("t p c -> p t c")))
        emit_cast_load(p, pooledT, pooledT_d, D, 64)
        emit_cast_load(p, rkvw, rkvw_d, D, 512)
        emit_cast_load(p, routw, routw_d, 256, D)
        emit_cast_load(p, outw, outw_d, D, D)
        p.gp(lambda e: e.dma_start(out=outb[:], in_=outb_d[:]))
        p.gp(lambda e: e.dma_start(out=glr[:], in_=glr_d[:]))
        p.sy(lambda e: e.dma_start(out=rkvbT[:], in_=rkvbT_d[:]))
        p.sy(lambda e: e.dma_start(out=routbT[:], in_=routbT_d[:]))
        p.sy(lambda e: e.dma_start(out=ones_row[:], in_=ones_d[:]))
        p.sy(lambda e: e.dma_start(out=ident[:], in_=ident_d[:]))
        p.ve(lambda e: e.memset(mcsh[:], -8.0))
        p.bar()

        for oc in range(4):
            for dc in range(NCH):
                p.te(lambda e, oc=oc, dc=dc: e.matmul(
                    pl[:, 0:64], rkvw[:, dc * 512 + oc * P: dc * 512 + oc * P + P],
                    pooledT[:, dc * 64:(dc + 1) * 64],
                    start=(dc == 0), stop=(dc == NCH - 1)))
            p.bar()
            p.ve(lambda e, oc=oc: e.tensor_scalar_add(
                rkv_fm[:, oc * 64:(oc + 1) * 64], pl[:, 0:64], rkvbT[:, oc:oc + 1]))
            p.bar()
        for lh in range(4):
            ro = (lh % 2) * 64
            cc = (2 + lh // 2) * 64
            p.te(lambda e, ro=ro, cc=cc: e.transpose(
                ptb[0:64, 0:64], rkv_fm[ro:ro + 64, cc:cc + 64], ident[ro:ro + 64, ro:ro + 64]))
            p.bar()
            p.ve(lambda e, lh=lh: e.tensor_copy(rvT[0:64, lh * 64:(lh + 1) * 64], ptb[0:64, 0:64]))
            p.bar()

        for t in range(NT):
            for lh in range(4):
                rqc = rq_sb[lh // 2]
                ro = (lh % 2) * 64
                p.te(lambda e, rqc=rqc, ro=ro, t=t, lh=lh: e.matmul(
                    pl[:, 0:64], rqc[ro:ro + 64, t * P:(t + 1) * P],
                    rkv_fm[ro:ro + 64, (lh // 2) * 64:(lh // 2) * 64 + 64],
                    start=True, stop=True))
                p.bar()
                p.ve(lambda e, t=t: e.scalar_tensor_tensor(
                    zrow[:], pl[:, 0:64], 0.125, lmask[:, t * 64:(t + 1) * 64], ALU.mult, ALU.add))
                p.bar()
                p.ve(lambda e: e.reduce_max(zmx[:], zrow[:], AX.X))
                p.bar()
                p.ve(lambda e: e.tensor_scalar_max(zmx[:], zmx[:], -80.0))
                p.bar()
                p.ve(lambda e: e.tensor_scalar_mul(zmx[:], zmx[:], -1.0))
                p.bar()
                p.se(lambda e: e.activation(Pl[:], zrow[:], AF.Exp, bias=zmx[:]))
                p.bar()
                p.ve(lambda e: e.reduce_sum(zsum[:], Pl[:], AX.X))
                p.bar()
                p.ve(lambda e: e.tensor_scalar_max(zsum[:], zsum[:], 1e-30))
                p.bar()
                p.ve(lambda e: e.reciprocal(zsum[:], zsum[:]))
                p.bar()
                p.ve(lambda e: e.tensor_scalar(Pl[:], Pl[:], zsum[:], None, ALU.mult))
                p.bar()
                p.te(lambda e: e.transpose(ptb[0:64, :], Pl[:], ident[:]))
                p.bar()
                p.ve(lambda e: e.tensor_copy(PnT[0:64, :], ptb[0:64, :]))
                p.bar()
                p.te(lambda e, lh=lh: e.matmul(
                    pl[0:64, 128:256], rvT[0:64, lh * 64:(lh + 1) * 64], PnT[0:64, :],
                    start=True, stop=True))
                p.bar()
                p.ve(lambda e, lh=lh, t=t, ro=ro: e.tensor_copy(
                    latfm[lh // 2][ro:ro + 64, t * P:(t + 1) * P], pl[0:64, 128:256]))
                p.bar()

        for oc in range(NCH):
            for tt in range(4):
                for lc in range(2):
                    p.te(lambda e, oc=oc, tt=tt, lc=lc: e.matmul(
                        pz[:, 0:512], routw[:, lc * D + oc * P: lc * D + oc * P + P],
                        latfm[lc][:, tt * 512:(tt + 1) * 512],
                        start=(lc == 0), stop=(lc == 1)))
                p.bar()
                p.ve(lambda e, oc=oc, tt=tt: e.tensor_scalar_add(
                    rem[oc][:, tt * 512:(tt + 1) * 512], pz[:, 0:512], routbT[:, oc:oc + 1]))
                p.bar()

        for hh in range(2):
            p.sy(lambda e, hh=hh: e.dma_start(
                out=vau[:].rearrange("p (h b c) -> p h b c", h=8, b=17),
                in_=vauge_d[hh * 8:(hh + 1) * 8].rearrange("h b p c -> p h b c")))
            p.bar()
            for hl in range(8):
                h = hh * 8 + hl
                ch, ro = h // 2, (h % 2) * 64
                for qb in range(8):
                    for sub in range(3):
                        eb = 2 * qb + sub
                        reg = pw[:, sub * 256:(sub + 1) * 256]
                        p.te(lambda e, reg=reg, ch=ch, ro=ro, eb=eb, qb=qb: e.matmul(
                            reg, kTe[ch][ro:ro + 64, eb * P:(eb + 1) * P],
                            q_sb[ch][ro:ro + 64, qb * 256:(qb + 1) * 256],
                            start=True, stop=True))
                    p.bar()
                    p.ve(lambda e, qb=qb: e.scalar_tensor_tensor(
                        pw[:, 0:768], pw[:, 0:768], 0.125,
                        masks[:, qb * 768:(qb + 1) * 768], ALU.mult, ALU.add))
                    p.bar()
                    p.se(lambda e: e.activation(Pw[:], pw[:, 0:768], AF.Exp, bias=mcsh[:]))
                    p.bar()
                    for sub in range(3):
                        eb = 2 * qb + sub
                        p.te(lambda e, sub=sub, eb=eb, hl=hl: e.matmul(
                            pav[:, 0:256], vau[:, (hl * 17 + eb) * 65:(hl * 17 + eb + 1) * 65],
                            Pw[:, sub * 256:(sub + 1) * 256],
                            start=(sub == 0), stop=(sub == 2)))
                    p.bar()
                    p.ve(lambda e: e.tensor_copy(srow[0:1, 0:256], pav[64:65, 0:256]))
                    p.bar()
                    p.ve(lambda e: e.tensor_scalar_max(srow[0:1, 0:256], srow[0:1, 0:256], 1e-30))
                    p.bar()
                    p.ve(lambda e: e.reciprocal(srow[0:1, 0:256], srow[0:1, 0:256]))
                    p.bar()
                    p.ve(lambda e: e.tensor_copy(rbf[0:1, 0:256], srow[0:1, 0:256]))
                    p.bar()
                    p.te(lambda e: e.matmul(pz[0:64, 0:256], ones_row[0:1, 0:64], rbf[0:1, 0:256],
                                            start=True, stop=True))
                    p.bar()
                    p.ve(lambda e: e.tensor_copy(rb_sb[0:64, 0:256], pz[0:64, 0:256]))
                    p.bar()
                    p.ve(lambda e, hl=hl, qb=qb: e.tensor_tensor(
                        yh[hl // 2][(hl % 2) * 64:(hl % 2) * 64 + 64, qb * 256:(qb + 1) * 256],
                        pav[0:64, 0:256], rb_sb[0:64, 0:256], ALU.mult))
                    p.bar()
                for tt in range(4):
                    gb = (hl % 2) * 64
                    ysl = yh[hl // 2][gb:gb + 64, tt * 512:(tt + 1) * 512]
                    rsl = rem[ch][ro:ro + 64, tt * 512:(tt + 1) * 512]
                    p.te(lambda e, ysl=ysl, gb=gb: e.matmul(
                        pl[0:1, 0:512], glr[gb:gb + 64, 0:1], ysl, start=True, stop=False))
                    p.te(lambda e, rsl=rsl, ro=ro: e.matmul(
                        pl[0:1, 0:512], glr[ro:ro + 64, 1:2], rsl, start=False, stop=True))
                    p.bar()
                    p.se(lambda e: e.activation(grow[0:1, :], pl[0:1, 0:512], AF.Sigmoid))
                    p.bar()
                    p.ve(lambda e: e.tensor_copy(grbf[0:1, :], grow[0:1, :]))
                    p.bar()
                    p.te(lambda e: e.matmul(pz[:, 0:512], ones_row[0:1, :], grbf[0:1, :],
                                            start=True, stop=True))
                    p.bar()
                    p.ve(lambda e: e.tensor_copy(rb_sb[:], pz[:, 0:512]))
                    p.bar()
                    p.ve(lambda e, ysl=ysl, rsl=rsl, ro=ro: e.tensor_tensor(gd[ro:ro + 64, :], ysl, rsl, ALU.subtract))
                    p.bar()
                    p.ve(lambda e, ro=ro: e.tensor_tensor(gd[ro:ro + 64, :], gd[ro:ro + 64, :], rb_sb[ro:ro + 64, :], ALU.mult))
                    p.bar()
                    p.ve(lambda e, rsl=rsl, ro=ro: e.tensor_tensor(rsl, rsl, gd[ro:ro + 64, :], ALU.add))
                    p.bar()

        for t in range(NT):
            for half in range(2):
                reg = pz[:, half * 512:(half + 1) * 512]
                for c in range(NCH):
                    p.te(lambda e, reg=reg, c=c, t=t, half=half: e.matmul(
                        reg, rem[c][:, t * P:(t + 1) * P],
                        outw[:, c * D + half * 512: c * D + (half + 1) * 512],
                        start=(c == 0), stop=False))
                p.te(lambda e, reg=reg, half=half: e.matmul(
                    reg, ones_row[:], outb[0:1, half * 512:(half + 1) * 512], start=False, stop=True))
                p.bar()
                p.ve(lambda e, reg=reg, half=half: e.tensor_copy(s_f32[:, half * 512:(half + 1) * 512], reg))
                p.bar()
            p.se_dma(lambda e, t=t: e.dma_start(out=s_d[t * P:(t + 1) * P, :], in_=s_f32[:]))
            p.bar()

        p.build(block, sem)
    return nc


def emit_quant(p, src, dst, ax, acc1, acc2, amax, rinv, nblk, sgn=None, t7=None):
    bv = lambda a: a.rearrange("p (b c) -> p b c", c=16)
    cv = lambda a: a.rearrange("p (b u) -> p b u", u=1)
    if sgn is None:
        sgn = acc1
    p.ve(lambda e: e.tensor_reduce(cv(amax), bv(src), AX.X, ALU.max, apply_absolute_value=True))
    p.se(lambda e: e.activation(ax, src, AF.Abs))
    p.bar()
    p.ve(lambda e: e.tensor_scalar(amax, amax, 1.0 / 6.0, 1e-12, ALU.mult, ALU.max))
    p.se(lambda e: e.activation(sgn, src, AF.Sign))
    p.bar()
    p.ve(lambda e: e.reciprocal(rinv, amax))
    p.bar()
    p.ve(lambda e: e.tensor_tensor(bv(ax), bv(ax), cv(rinv).broadcast_to([P, nblk, 16]), ALU.mult))
    p.gp(lambda e: e.tensor_tensor(bv(sgn), bv(sgn), cv(amax).broadcast_to([P, nblk, 16]), ALU.mult)) \
        if False else p.on("gpsimd", lambda e: e.tensor_tensor(
            bv(sgn), bv(sgn), cv(amax).broadcast_to([P, nblk, 16]), ALU.mult))
    p.bar()
    p.ve(lambda e: e.tensor_single_scalar(acc1, ax, 0.25, ALU.is_gt))
    p.ve(lambda e: e.tensor_single_scalar(acc2, ax, 2.5, ALU.is_gt))
    p.bar()
    p.ve(lambda e: e.scalar_tensor_tensor(acc1, ax, 0.75, acc1, ALU.is_gt, ALU.add))
    p.bar()
    p.ve(lambda e: e.scalar_tensor_tensor(acc2, ax, 3.5, acc2, ALU.is_gt, ALU.add))
    if t7 is not None:
        p.ve(lambda e: e.tensor_single_scalar(t7, ax, 5.0, ALU.is_gt))
    p.bar()
    p.ve(lambda e: e.scalar_tensor_tensor(acc1, ax, 1.25, acc1, ALU.is_gt, ALU.add))
    p.bar()
    p.ve(lambda e: e.scalar_tensor_tensor(acc1, ax, 1.75, acc1, ALU.is_gt, ALU.add))
    p.bar()
    if t7 is None:
        p.ve(lambda e: e.scalar_tensor_tensor(acc2, acc1, 0.5, acc2, ALU.mult, ALU.add))
        p.bar()
        p.ve(lambda e: e.tensor_single_scalar(acc1, ax, 5.0, ALU.is_gt))
        p.bar()
        p.ve(lambda e: e.scalar_tensor_tensor(acc2, acc1, 2.0, acc2, ALU.mult, ALU.add))
        p.bar()
    else:
        p.ve(lambda e: e.scalar_tensor_tensor(acc2, acc1, 0.5, acc2, ALU.mult, ALU.add))
        p.bar()
        p.ve(lambda e: e.scalar_tensor_tensor(acc2, t7, 2.0, acc2, ALU.mult, ALU.add))
        p.bar()
    p.ve(lambda e: e.tensor_tensor(dst, acc2, sgn, ALU.mult))
    p.bar()


def emit_tr4(p, pairs, ptw, ident_or_f, bf16):
    for i0 in range(0, len(pairs), 4):
        grp = pairs[i0:i0 + 4]
        for k, (src, dstp) in enumerate(grp):
            p.te(lambda e, k=k, src=src: e.transpose(ptw[:, k * P:(k + 1) * P], src, ident_or_f[:]))
        p.bar()
        for k, (src, dstp) in enumerate(grp):
            p.ve(lambda e, k=k, dstp=dstp: e.tensor_copy(dstp, ptw[:, k * P:(k + 1) * P]))
        p.bar()


def build_b2c():
    nc = bass.Bass("TRN2", target_bir_lowering=False)
    dd = lambda n, s_, dt=F32, k="ExternalInput": nc.dram_tensor(n, s_, dt, kind=k).ap()
    x_d = dd("x", [TPC, D])
    s_in_d = dd("s", [TPC, D])
    xn_d = dd("xn", [TPC, D], BF16)
    wg_d = dd("w_gate", [2816, D])
    wu_d = dd("w_up", [2816, D])
    wd_d = dd("w_down", [D, 2816])
    ident_d = dd("ident", [P, P], BF16)
    identf_d = dd("identf", [P, P])
    xo_d = dd("xo", [TPC, D], k="ExternalOutput")
    wgT_dram = dd("wgT_scr", [D, 2816], BF16, k="Internal")
    wuT_dram = dd("wuT_scr", [D, 2816], BF16, k="Internal")

    NI = 2816
    NIC = NI // P
    GT = 1024
    NG = TPC // GT
    p = Prog()
    with contextlib.ExitStack() as ctx:
        sb = lambda n, s_, dt=F32: ctx.enter_context(nc.sbuf_tensor(n, s_, dt))
        psm = lambda n, s_, dt=F32: ctx.enter_context(nc.psum_tensor(n, s_, dt))
        xq_fm = [sb(f"xqfm{c}", [P, TPC], BF16) for c in range(NCH)]
        wdqT = [sb(f"wdqT{i}", [P, D], BF16) for i in range(NIC)]
        hq_fm = [sb(f"hqfm{i}", [P, GT], BF16) for i in range(NIC)]
        wload = sb("wload", [P, NI])
        wq_bf = sb("wq_bf", [P, NI], BF16)
        qax = sb("qax", [P, D])
        qacc1 = sb("qacc1", [P, D])
        qacc2 = sb("qacc2", [P, D])
        qsgn = sb("qsgn", [P, D])
        qt7 = sb("qt7", [P, D])
        qamax = sb("qamax", [P, 64])
        qrinv = sb("qrinv", [P, 64])
        qdst = sb("qdst", [P, D], BF16)
        wgT_c = sb("wgT_c", [P, NCH * P], BF16)
        wuT_c = sb("wuT_c", [P, NCH * P], BF16)
        gsb = sb("gsb", [P, GT])
        hidf = sb("hidf", [P, GT])
        hq_tm = sb("hq_tm", [P, GT])
        ident = sb("ident_s", [P, P], BF16)
        identf = sb("identf_s", [P, P])
        xt_ = sb("xt_", [P, D])
        st_ = sb("st_", [P, D])
        gt_ = sb("gt_", [P, D])
        pA = psm("pA", [P, 2 * 512])
        pB = psm("pB", [P, 2 * 512])
        pC = psm("pC", [P, 2 * 512])
        ptb = psm("ptb", [P, 4 * P], BF16)
        ptf = psm("ptf", [P, 4 * P])
        sem = ctx.enter_context(nc.semaphore("sem"))
        block = ctx.enter_context(nc.Block())

        p.sy(lambda e: e.dma_start(out=ident[:], in_=ident_d[:]))
        p.sy(lambda e: e.dma_start(out=identf[:], in_=identf_d[:]))
        p.bar()

        for w_src, w_dst in ((wg_d, wgT_dram), (wu_d, wuT_dram)):
            for ic in range(NIC):
                p.sy(lambda e, w_src=w_src, ic=ic: e.dma_start(
                    out=wload[:, 0:D], in_=w_src[ic * P:(ic + 1) * P, :]))
                p.bar()
                emit_quant(p, wload[:, 0:D], qdst[:], qax[:], qacc1[:], qacc2[:],
                           qamax[:], qrinv[:], 64, sgn=qsgn[:], t7=qt7[:])
                emit_tr4(p, [(qdst[:, dc * P:(dc + 1) * P], wgT_c[:, dc * P:(dc + 1) * P])
                             for dc in range(NCH)], ptb, ident, True)
                p.se_dma(lambda e, w_dst=w_dst, ic=ic: e.dma_start(
                    out=w_dst.rearrange("(c p) i -> p c i", p=P)[:, :, ic * P:(ic + 1) * P],
                    in_=wgT_c[:].rearrange("p (c i) -> p c i", c=NCH)))
                p.bar()

        for dc in range(NCH):
            p.sy(lambda e, dc=dc: e.dma_start(out=wload[:], in_=wd_d[dc * P:(dc + 1) * P, :]))
            p.bar()
            for pc in range(3):
                n = 1024 if pc < 2 else 768
                emit_quant(p, wload[:, pc * 1024:pc * 1024 + n], qdst[:, 0:n],
                           qax[:, 0:n], qacc1[:, 0:n], qacc2[:, 0:n],
                           qamax[:, 0:n // 16], qrinv[:, 0:n // 16], n // 16,
                           sgn=qsgn[:, 0:n], t7=qt7[:, 0:n])
                emit_tr4(p, [(qdst[:, k * P:(k + 1) * P], wdqT[pc * 8 + k][:, dc * P:(dc + 1) * P])
                             for k in range(n // P)], ptb, ident, True)

        for t in range(NT):
            p.sy(lambda e, t=t: e.dma_start(out=qdst[:], in_=xn_d[t * P:(t + 1) * P, :]))
            p.bar()
            emit_quant(p, qdst[:], qdst[:], qax[:], qacc1[:], qacc2[:], qamax[:], qrinv[:], 64,
                       sgn=qsgn[:], t7=qt7[:])
            emit_tr4(p, [(qdst[:, dc * P:(dc + 1) * P], xq_fm[dc][:, t * P:(t + 1) * P])
                         for dc in range(NCH)], ptb, ident, True)

        for g in range(NG):
            for ic in range(NIC):
                p.sy(lambda e, ic=ic: e.dma_start(
                    out=wgT_c[:].rearrange("p (c i) -> p c i", c=NCH),
                    in_=wgT_dram.rearrange("(c p) i -> p c i", p=P)[:, :, ic * P:(ic + 1) * P]))
                p.sy(lambda e, ic=ic: e.dma_start(
                    out=wuT_c[:].rearrange("p (c i) -> p c i", c=NCH),
                    in_=wuT_dram.rearrange("(c p) i -> p c i", p=P)[:, :, ic * P:(ic + 1) * P]))
                p.bar()
                for t5 in range(2):
                    for dc in range(NCH):
                        p.te(lambda e, t5=t5, dc=dc, g=g: e.matmul(
                            pA[:, t5 * 512:(t5 + 1) * 512], wgT_c[:, dc * P:(dc + 1) * P],
                            xq_fm[dc][:, g * GT + t5 * 512: g * GT + (t5 + 1) * 512],
                            start=(dc == 0), stop=(dc == NCH - 1)))
                        p.te(lambda e, t5=t5, dc=dc, g=g: e.matmul(
                            pB[:, t5 * 512:(t5 + 1) * 512], wuT_c[:, dc * P:(dc + 1) * P],
                            xq_fm[dc][:, g * GT + t5 * 512: g * GT + (t5 + 1) * 512],
                            start=(dc == 0), stop=(dc == NCH - 1)))
                p.bar()
                for t5 in range(2):
                    p.se(lambda e, t5=t5: e.activation(gsb[:, t5 * 512:(t5 + 1) * 512],
                                                       pA[:, t5 * 512:(t5 + 1) * 512], AF.Silu))
                p.bar()
                for t5 in range(2):
                    p.ve(lambda e, t5=t5: e.tensor_tensor(
                        hidf[:, t5 * 512:(t5 + 1) * 512], pB[:, t5 * 512:(t5 + 1) * 512],
                        gsb[:, t5 * 512:(t5 + 1) * 512], ALU.mult))
                p.bar()
                emit_tr4(p, [(hidf[:, st * P:(st + 1) * P], hq_tm[:, st * P:(st + 1) * P])
                             for st in range(8)], ptf, identf, False)
                emit_quant(p, hq_tm[:], qdst[:], qax[:], qacc1[:], qacc2[:], qamax[:], qrinv[:], 64,
                           sgn=qsgn[:], t7=qt7[:])
                emit_tr4(p, [(qdst[:, st * P:(st + 1) * P], hq_fm[ic][:, st * P:(st + 1) * P])
                             for st in range(8)], ptb, ident, True)
            for t0 in (0, 3, 6):
                tl = list(range(t0, min(t0 + 3, 8)))
                for i, t in enumerate(tl):
                    pb2 = (pA, pB, pC)[i]
                    for half in range(2):
                        for ic in range(NIC):
                            p.te(lambda e, pb2=pb2, half=half, ic=ic, t=t: e.matmul(
                                pb2[:, half * 512:(half + 1) * 512], hq_fm[ic][:, t * P:(t + 1) * P],
                                wdqT[ic][:, half * 512:(half + 1) * 512],
                                start=(ic == 0), stop=(ic == NIC - 1)))
                p.bar()
                for i, t in enumerate(tl):
                    pb2 = (pA, pB, pC)[i]
                    tg = g * (GT // P) + t
                    p.sy(lambda e, tg=tg: e.dma_start(out=xt_[:], in_=x_d[tg * P:(tg + 1) * P, :]))
                    p.sy(lambda e, tg=tg: e.dma_start(out=st_[:], in_=s_in_d[tg * P:(tg + 1) * P, :]))
                    p.bar()
                    for half in range(2):
                        p.se(lambda e, pb2=pb2, half=half: e.activation(
                            gt_[:, half * 512:(half + 1) * 512], pb2[:, half * 512:(half + 1) * 512], AF.Silu))
                    p.bar()
                    p.ve(lambda e: e.tensor_tensor(gt_[:], st_[:], gt_[:], ALU.mult))
                    p.bar()
                    p.ve(lambda e: e.tensor_tensor(gt_[:], xt_[:], gt_[:], ALU.add))
                    p.bar()
                    p.se_dma(lambda e, tg=tg: e.dma_start(out=xo_d[tg * P:(tg + 1) * P, :], in_=gt_[:]))
                    p.bar()

        p.build(block, sem)
    return nc


def build_b3a():
    nc = bass.Bass("TRN2", target_bir_lowering=False)
    dd = lambda n, s_, dt=F32, k="ExternalInput": nc.dram_tensor(n, s_, dt, kind=k).ap()
    x_d = dd("x_ext", [TPC + P, D])
    wqT_d = dd("WqT", [D, D])
    wkT_d = dd("WkT", [D, D])
    wvT_d = dd("WvT", [D, D])
    wgT_d = dd("WgT", [D, D])
    convT_d = dd("convT", [P, 3 * 32])
    sel2_d = dd("sel2", [2, P], BF16)
    bd2_d = dd("bd2", [P, 2], BF16)
    ident_d = dd("ident", [P, P], BF16)
    xnT_d = dd("xnT", [D, TPC], BF16, k="ExternalOutput")
    qT_d = dd("qT", [D, TPC], BF16, k="ExternalOutput")
    kT_d = dd("kT", [D, TPC], BF16, k="ExternalOutput")
    vT_d = dd("vT", [D, TPC], BF16, k="ExternalOutput")
    gate_d = dd("gate", [TPC, D], BF16, k="ExternalOutput")
    xnscr_d = dd("xnscr", [TPC + P, D], BF16, k="Internal")

    NTE = NT + 1
    TE = TPC + P
    p = Prog()
    with contextlib.ExitStack() as ctx:
        sb = lambda n, s_, dt=F32: ctx.enter_context(nc.sbuf_tensor(n, s_, dt))
        psm = lambda n, s_, dt=F32: ctx.enter_context(nc.psum_tensor(n, s_, dt))
        xt = [sb(f"xt{i}", [P, D]) for i in range(2)]
        sq = sb("sq", [P, D])
        lncols = [[sb(f"ln{i}{n}", [P, 1]) for n in ("mc", "vc", "rc", "sr")] for i in range(2)]
        epscol = sb("epscol", [P, 1])
        mhalf = sb("mhalf", [P, 1])
        xn_bf2 = [sb(f"xnbf{i}", [P, D], BF16) for i in range(2)]
        sel2 = sb("sel2_s", [2, P], BF16)
        bd2 = sb("bd2_s", [P, 2], BF16)
        convT = sb("convT_s", [P, 3 * 32])
        xn_fm = [sb(f"xnfm{c}", [P, TE], BF16) for c in range(NCH)]
        pre = [sb(f"pre{c}", [P, TE]) for c in range(NCH)]
        w_sb = sb("w_sb", [P, NCH * D], BF16)
        fmout = [sb(f"fmo{c}", [P, TPC], BF16) for c in range(NCH)]
        cscr = sb("cscr", [P, TPC])
        silbf = sb("silbf", [P, TPC], BF16)
        sqbf = sb("sqbf", [P, TPC], BF16)
        nrm = sb("nrm", [2, TPC])
        nrmbf = sb("nrmbf", [2, TPC], BF16)
        rb = sb("rb", [P, 512])
        stg = sb("stg", [P, D], BF16)
        pk = psm("pk", [P, 4 * 512])
        pn2 = psm("pn2", [2, 512])
        sem = ctx.enter_context(nc.semaphore("sem"))
        block = ctx.enter_context(nc.Block())

        p.sy(lambda e: e.dma_start(out=sel2[:], in_=sel2_d[:]))
        p.sy(lambda e: e.dma_start(out=bd2[:], in_=bd2_d[:]))
        p.sy(lambda e: e.dma_start(out=convT[:], in_=convT_d[:]))
        p.sy(lambda e: e.dma_start(out=xt[0][:], in_=x_d[0:P, :]))
        p.ve(lambda e: e.memset(epscol[:], 1e-5))
        p.ve(lambda e: e.memset(mhalf[:], -0.5))
        p.bar()

        lnbufs = dict(sq=sq, cols=lncols, eps=epscol, mhalf=mhalf, xn_bf=xn_bf2)
        emit_ln_stream_to_fm(p, None, x_d, NTE, xt, lnbufs, xn_fm, xnscr_d)
        for c in range(NCH):
            p.se_dma(lambda e, c=c: e.dma_start(out=xnT_d[c * P:(c + 1) * P, :], in_=xn_fm[c][:, P:]))
        p.bar()

        def proj_ext(w_d, dst, ncols=TE):
            emit_cast_load(p, w_sb, w_d, D, D)
            p.bar()
            ntt = ncols // 512
            rem_c = ncols - ntt * 512
            for oc in range(NCH):
                for tt in range(ntt + (1 if rem_c else 0)):
                    n = 512 if tt < ntt else rem_c
                    for dc in range(NCH):
                        p.te(lambda e, oc=oc, tt=tt, dc=dc, n=n: e.matmul(
                            pk[:, 0:n], w_sb[:, dc * D + oc * P: dc * D + oc * P + P],
                            xn_fm[dc][:, tt * 512: tt * 512 + n],
                            start=(dc == 0), stop=(dc == NCH - 1)))
                    p.bar()
                    p.ve(lambda e, oc=oc, tt=tt, n=n: e.tensor_copy(
                        dst[oc][:, tt * 512: tt * 512 + n], pk[:, 0:n]))
                    p.bar()

        def conv_norm(widx, do_norm, fm_store, tm_out, fmT_out):
            for dc in range(NCH):
                p.ve(lambda e, dc=dc: e.tensor_copy(cscr[:], pre[dc][:, P:]))
                p.bar()
                for j in range(4):
                    wc = convT[:, widx * 32 + dc * 4 + j: widx * 32 + dc * 4 + j + 1]
                    p.ve(lambda e, dc=dc, j=j, wc=wc: e.scalar_tensor_tensor(
                        cscr[:], pre[dc][:, P - 3 + j: P - 3 + j + TPC], wc, cscr[:],
                        ALU.mult, ALU.add))
                    p.bar()
                p.se(lambda e: e.activation(silbf[:], cscr[:], AF.Silu))
                p.bar()
                if not do_norm:
                    p.ve(lambda e, dc=dc: e.tensor_copy(fm_store[dc][:], silbf[:]))
                    p.bar()
                    continue
                p.se(lambda e: e.activation(sqbf[:], silbf[:], AF.Square))
                p.bar()
                for sl in range(4):
                    p.te(lambda e, sl=sl: e.matmul(
                        pn2[:], bd2[:],
                        sqbf[:, sl * 512:(sl + 1) * 512], start=True, stop=True))
                    p.bar()
                    p.ve(lambda e, sl=sl: e.tensor_scalar_add(
                        nrm[:, sl * 512:(sl + 1) * 512], pn2[:], 1e-12))
                    p.bar()
                p.se(lambda e: e.activation(nrm[:], nrm[:], AF.Ln))
                p.bar()
                p.se(lambda e: e.activation(nrmbf[:], nrm[:], AF.Exp, scale=mhalf[0:2, 0:1]))
                p.bar()
                for sl in range(4):
                    p.te(lambda e, sl=sl: e.matmul(
                        pk[:, 0:512], sel2[:], nrmbf[:, sl * 512:(sl + 1) * 512],
                        start=True, stop=True))
                    p.bar()
                    p.ve(lambda e: e.tensor_copy(rb[:], pk[:, 0:512]))
                    p.bar()
                    p.ve(lambda e, dc=dc, sl=sl: e.tensor_tensor(
                        fm_store[dc][:, sl * 512:(sl + 1) * 512],
                        silbf[:, sl * 512:(sl + 1) * 512], rb[:], ALU.mult))
                    p.bar()
            if fmT_out is not None:
                for c in range(NCH):
                    p.se_dma(lambda e, c=c: e.dma_start(out=fmT_out[c * P:(c + 1) * P, :], in_=fm_store[c][:]))
                p.bar()
            assert tm_out is None

        proj_ext(wqT_d, pre)
        conv_norm(0, True, fmout, None, qT_d)
        proj_ext(wkT_d, pre)
        conv_norm(1, True, fmout, None, kT_d)
        proj_ext(wvT_d, pre)
        conv_norm(2, False, fmout, None, vT_d)

        emit_cast_load(p, w_sb, wgT_d, D, D)
        p.bar()
        for t in range(NT):
            for half in range(2):
                for dc in range(NCH):
                    p.te(lambda e, t=t, half=half, dc=dc: e.matmul(
                        pk[:, half * 512:(half + 1) * 512],
                        xn_fm[dc][:, P + t * P: P + (t + 1) * P],
                        w_sb[:, dc * D + half * 512: dc * D + (half + 1) * 512],
                        start=(dc == 0), stop=(dc == NCH - 1)))
            p.bar()
            for half in range(2):
                p.ve(lambda e, half=half: e.tensor_copy(stg[:, half * 512:(half + 1) * 512],
                                                        pk[:, half * 512:(half + 1) * 512]))
            p.bar()
            p.se_dma(lambda e, t=t: e.dma_start(out=gate_d[t * P:(t + 1) * P, :], in_=stg[:]))
            p.bar()

        p.build(block, sem)
    return nc


def build_b3b():
    nc = bass.Bass("TRN2", target_bir_lowering=False)
    dd = lambda n, s_, dt=F32, k="ExternalInput": nc.dram_tensor(n, s_, dt, kind=k).ap()
    qT_dd = dd("qT_bh", [4, 64, S], BF16)
    kT_dd = dd("kT_bh", [4, 64, S], BF16)
    ktm_dd = dd("k_bh", [4, S, 64], BF16)
    vtm_dd = dd("v_bh", [4, S, 64], BF16)
    wi_d = dd("Winit", [4, 64, 64])
    tri_d = dd("tri_su", [P, P], BF16)
    gam_d = dd("gam_bc", [P, 64])
    bet_d = dd("bet_bc", [P, 64])
    o_d = dd("o_bh", [4, S, 64], BF16, k="ExternalOutput")

    NCHK = S // P
    p = Prog()
    with contextlib.ExitStack() as ctx:
        sb = lambda n, s_, dt=F32: ctx.enter_context(nc.sbuf_tensor(n, s_, dt))
        psm = lambda n, s_, dt=F32: ctx.enter_context(nc.psum_tensor(n, s_, dt))
        qT = [sb(f"qT{i}", [64, S], BF16) for i in range(2)]
        kT = [sb(f"kT{i}", [64, S], BF16) for i in range(2)]
        ktm = [sb(f"ktm{i}", [P, NCHK * 64], BF16) for i in range(2)]
        vtm = [sb(f"vtm{i}", [P, NCHK * 64], BF16) for i in range(2)]
        W0 = [sb(f"W0_{i}", [64, 64]) for i in range(2)]
        W_sb = [sb(f"Wsb{i}", [64, 64], BF16) for i in range(2)]
        tri = sb("tri_s", [P, P], BF16)
        gam8 = sb("gam8", [P, 64])
        bet = sb("bet_s", [P, 64])
        mqk = sb("mqk", [P, 2 * P], BF16)
        osc = [sb(f"osc{i}", [P, 64]) for i in range(2)]
        osq = sb("osq", [P, 64])
        mu = [sb(f"mu{i}", [P, 1]) for i in range(2)]
        va = [sb(f"va{i}", [P, 1]) for i in range(2)]
        rs = [sb(f"rs{i}", [P, 1]) for i in range(2)]
        obf = [sb(f"obf{i}", [P, 64], BF16) for i in range(2)]
        epsc = sb("epsc", [P, 1])
        mh2 = sb("mh2", [P, 1])
        po = psm("po", [P, 4 * 512])
        pqk = psm("pqk", [P, 2 * 512])
        pW = psm("pW", [64, 2 * 512])
        sem = ctx.enter_context(nc.semaphore("sem"))
        block = ctx.enter_context(nc.Block())

        p.sy(lambda e: e.dma_start(out=tri[:], in_=tri_d[:]))
        p.sy(lambda e: e.dma_start(out=gam8[:], in_=gam_d[:]))
        p.sy(lambda e: e.dma_start(out=bet[:], in_=bet_d[:]))
        p.ve(lambda e: e.memset(epsc[:], 64.0e-5))
        p.ve(lambda e: e.memset(mh2[:], -0.5))
        p.bar()
        p.ve(lambda e: e.tensor_scalar_mul(gam8[:], gam8[:], 8.0))
        p.bar()

        for duo in range(2):
            for i in range(2):
                pr = duo * 2 + i
                p.sy(lambda e, pr=pr, i=i: e.dma_start(out=qT[i][:], in_=qT_dd[pr]))
                p.sy(lambda e, pr=pr, i=i: e.dma_start(out=kT[i][:], in_=kT_dd[pr]))
                p.sy(lambda e, pr=pr, i=i: e.dma_start(
                    out=ktm[i][:].rearrange("p (n c) -> p n c", c=64),
                    in_=ktm_dd[pr].rearrange("(n p) c -> p n c", p=P)))
                p.sy(lambda e, pr=pr, i=i: e.dma_start(
                    out=vtm[i][:].rearrange("p (n c) -> p n c", c=64),
                    in_=vtm_dd[pr].rearrange("(n p) c -> p n c", p=P)))
                p.sy(lambda e, pr=pr, i=i: e.dma_start(out=W0[i][:], in_=wi_d[pr]))
            p.bar()
            for i in range(2):
                p.ve(lambda e, i=i: e.tensor_copy(W_sb[i][:], W0[i][:]))
            p.bar()

            def emit_ln_stage(stage, m):
                par = m % 2
                for i in range(2):
                    reg = po[:, (2 * i) * 512:(2 * i) * 512 + 64]
                    if stage == 1:
                        p.ve(lambda e, i=i, reg=reg: e.reduce_sum(mu[i][:], reg, AX.X))
                        p.bar()
                        p.ve(lambda e, i=i: e.tensor_scalar_mul(mu[i][:], mu[i][:], 1.0 / 64))
                    elif stage == 2:
                        p.ve(lambda e, i=i, reg=reg: e.tensor_scalar(osc[i][:], reg, mu[i][:], None, ALU.subtract))
                    elif stage == 3:
                        p.se(lambda e, i=i: e.activation(osq[:], osc[i][:], AF.Square, accum_out=va[i][:]))
                        p.bar()
                        p.se(lambda e, i=i: e.activation(rs[i][:], va[i][:], AF.Ln, bias=epsc[:]))
                        p.se(lambda e, i=i: e.activation(rs[i][:], rs[i][:], AF.Exp, scale=mh2[:]))
                    elif stage == 4:
                        p.ve(lambda e, i=i: e.tensor_scalar(osc[i][:], osc[i][:], rs[i][:], None, ALU.mult))
                        p.ve(lambda e, i=i: e.tensor_tensor(osc[i][:], osc[i][:], gam8[:], ALU.mult))
                        p.ve(lambda e, i=i: e.tensor_tensor(obf[i][:], osc[i][:], bet[:], ALU.add))

            def emit_ln_out(m):
                par = m % 2
                for i in range(2):
                    pr = duo * 2 + i
                    p.se_dma(lambda e, i=i, pr=pr, m=m: e.dma_start(
                        out=o_d[pr, m * P:(m + 1) * P, :], in_=obf[i][:]))

            for n in range(NCHK):
                par = n % 2
                for i in range(2):
                    reg = po[:, (2 * i) * 512:(2 * i) * 512 + 64]
                    p.te(lambda e, i=i, n=n, reg=reg: e.matmul(
                        reg, qT[i][:, n * P:(n + 1) * P], W_sb[i][:], start=True, stop=False))
                    p.te(lambda e, i=i, n=n: e.matmul(
                        pqk[:, i * 512:i * 512 + P], kT[i][:, n * P:(n + 1) * P],
                        qT[i][:, n * P:(n + 1) * P], start=True, stop=True))
                p.bar()
                for i in range(2):
                    p.ve(lambda e, i=i: e.tensor_tensor(mqk[:, i * P:(i + 1) * P],
                                                        pqk[:, i * 512:i * 512 + P], tri[:], ALU.mult))
                p.bar()
                for i in range(2):
                    reg = po[:, (2 * i) * 512:(2 * i) * 512 + 64]
                    p.te(lambda e, i=i, n=n, reg=reg: e.matmul(
                        reg, mqk[:, i * P:(i + 1) * P], vtm[i][:, n * 64:(n + 1) * 64],
                        start=False, stop=True))
                    p.te(lambda e, i=i, n=n: e.matmul(
                        pW[:, i * 512:i * 512 + 64], ktm[i][:, n * 64:(n + 1) * 64],
                        vtm[i][:, n * 64:(n + 1) * 64],
                        start=(n == 0), stop=(n == NCHK - 1)))
                p.bar()
                for i in range(2):
                    p.ve(lambda e, i=i: e.tensor_tensor(W_sb[i][:], W0[i][:],
                                                        pW[:, i * 512:i * 512 + 64], ALU.add))
                emit_ln_stage(1, n)
                emit_ln_stage(2, n)
                p.bar()
                emit_ln_stage(3, n)
                p.bar()
                emit_ln_stage(4, n)
                p.bar()
                emit_ln_out(n)
                p.bar()

        p.build(block, sem)
    return nc


def build_b3c():
    nc = bass.Bass("TRN2", target_bir_lowering=False)
    dd = lambda n, s_, dt=F32, k="ExternalInput": nc.dram_tensor(n, s_, dt, kind=k).ap()
    x_d = dd("x", [TPC, D])
    xnT_d = dd("xnT", [D, TPC], BF16)
    o_dd = dd("o", [TPC, D], BF16)
    gate_dd = dd("gate", [TPC, D], BF16)
    woT_d = dd("WoT", [D, D])
    rtr_d = dd("moe_router", [D, 8])
    w13_d = dd("moe_w13", [8, D, 4096])
    w2_d = dd("moe_w2", [8, 2048, D])
    ident_d = dd("ident", [P, P], BF16)
    xo_d = dd("xo", [TPC, D], k="ExternalOutput")
    s_scr = dd("s_scr", [TPC, D], k="Internal")

    GT = 1024
    NG = TPC // GT
    NT_G = GT // P
    p = Prog()
    with contextlib.ExitStack() as ctx:
        sb = lambda n, s_, dt=F32: ctx.enter_context(nc.sbuf_tensor(n, s_, dt))
        psm = lambda n, s_, dt=F32: ctx.enter_context(nc.psum_tensor(n, s_, dt))
        xn_fm = [sb(f"xnfm{c}", [P, TPC], BF16) for c in range(NCH)]
        og_fm = [sb(f"ogfm{c}", [P, TPC], BF16) for c in range(NCH)]
        hid = [sb(f"hid{c}", [P, GT], BF16) for c in range(16)]
        w2h = [sb(f"w2h{c}", [P, 512], BF16) for c in range(16)]
        moe = sb("moe", [P, NT_G * D])
        woT = sb("woT", [P, NCH * D], BF16)
        rtr = sb("rtr", [P, NCH * 8], BF16)
        routes = sb("routes_s", [P, NT * 8])
        m1 = sb("m1", [P, 1]); m2 = sb("m2", [P, 1])
        g1 = sb("g1", [P, 1]); g2 = sb("g2", [P, 1])
        hsc = sb("hsc", [P, 8])
        lgs = sb("lgs", [P, 8])
        ident = sb("ident_s", [P, P], BF16)
        ogt = sb("ogt", [P, D], BF16)
        w1buf = [sb(f"w1buf{i}", [P, 512], BF16) for i in range(2)]
        gsb = sb("gsb", [P, 512])
        xt_ = sb("xt_", [P, D]); st_ = sb("st_", [P, D]); gt_ = sb("gt_", [P, D])
        ptile = psm("ptile", [P, 4 * 512])
        pmid = psm("pmid", [P, 3 * 512])
        ptb = psm("ptb", [P, P], BF16)
        sem = ctx.enter_context(nc.semaphore("sem"))
        block = ctx.enter_context(nc.Block())

        p.sy(lambda e: e.dma_start(out=ident[:], in_=ident_d[:]))
        for c in range(NCH):
            p.sy(lambda e, c=c: e.dma_start(out=xn_fm[c][:], in_=xnT_d[c * P:(c + 1) * P, :]))
        emit_cast_load(p, woT, woT_d, D, D)
        emit_cast_load(p, rtr, rtr_d, D, 8)
        p.bar()

        gbf = sb("gbf", [P, D], BF16)
        for t in range(NT):
            p.sy(lambda e, t=t: e.dma_start(out=ogt[:], in_=o_dd[t * P:(t + 1) * P, :]))
            p.sy(lambda e, t=t: e.dma_start(out=gbf[:], in_=gate_dd[t * P:(t + 1) * P, :]))
            p.bar()
            p.ve(lambda e: e.tensor_tensor(ogt[:], ogt[:], gbf[:], ALU.mult))
            p.bar()
            for c in range(NCH):
                p.te(lambda e, c=c: e.transpose(ptb[:], ogt[:, c * P:(c + 1) * P], ident[:]))
                p.bar()
                p.ve(lambda e, c=c, t=t: e.tensor_copy(og_fm[c][:, t * P:(t + 1) * P], ptb[:]))
                p.bar()
        for t in range(NT):
            for half in range(2):
                for dc in range(NCH):
                    p.te(lambda e, t=t, half=half, dc=dc: e.matmul(
                        ptile[:, half * 512:(half + 1) * 512],
                        og_fm[dc][:, t * P:(t + 1) * P],
                        woT[:, dc * D + half * 512: dc * D + (half + 1) * 512],
                        start=(dc == 0), stop=(dc == NCH - 1)))
            p.bar()
            for half in range(2):
                p.ve(lambda e, half=half: e.tensor_copy(xt_[:, half * 512:(half + 1) * 512],
                                                        ptile[:, half * 512:(half + 1) * 512]))
            p.bar()
            p.se_dma(lambda e, t=t: e.dma_start(out=s_scr[t * P:(t + 1) * P, :], in_=xt_[:]))
            p.bar()

        for t in range(NT):
            for dc in range(NCH):
                p.te(lambda e, t=t, dc=dc: e.matmul(
                    ptile[:, 0:8], xn_fm[dc][:, t * P:(t + 1) * P], rtr[:, dc * 8:(dc + 1) * 8],
                    start=(dc == 0), stop=(dc == NCH - 1)))
            p.bar()
            p.ve(lambda e: e.tensor_copy(lgs[:], ptile[:, 0:8]))
            p.bar()
            p.ve(lambda e: e.reduce_max(m1[:], lgs[:], AX.X))
            p.bar()
            p.ve(lambda e: e.tensor_single_scalar(hsc[:], lgs[:], m1[:], ALU.is_equal))
            p.bar()
            p.ve(lambda e: e.scalar_tensor_tensor(hsc[:], hsc[:], -1e30, lgs[:], ALU.mult, ALU.add))
            p.bar()
            p.ve(lambda e: e.reduce_max(m2[:], hsc[:], AX.X))
            p.bar()
            p.ve(lambda e: e.tensor_tensor(g1[:], m1[:], m2[:], ALU.subtract))
            p.bar()
            p.se(lambda e: e.activation(g1[:], g1[:], AF.Sigmoid))
            p.bar()
            p.ve(lambda e: e.tensor_scalar(g2[:], g1[:], -1.0, 1.0, ALU.mult, ALU.add))
            p.bar()
            rt = routes[:, t * 8:(t + 1) * 8]
            p.ve(lambda e, rt=rt: e.tensor_single_scalar(rt, lgs[:], m1[:], ALU.is_equal))
            p.bar()
            p.ve(lambda e, rt=rt: e.tensor_scalar(rt, rt, g1[:], None, ALU.mult))
            p.bar()
            p.ve(lambda e: e.tensor_single_scalar(hsc[:], hsc[:], m2[:], ALU.is_equal))
            p.bar()
            p.ve(lambda e, rt=rt: e.scalar_tensor_tensor(hsc[:], hsc[:], g2[:], rt, ALU.mult, ALU.add))
            p.bar()
            p.ve(lambda e, rt=rt: e.tensor_copy(rt, hsc[:]))
            p.bar()

        for g in range(NG):
            p.ve(lambda e: e.memset(moe[:], 0.0))
            p.bar()
            for ex in range(8):
                for hc in range(16):
                    for part, coloff in ((0, 0), (1, 2048)):
                        for dc in range(NCH):
                            p.gp(lambda e, ex=ex, hc=hc, dc=dc, coloff=coloff: e.dma_start(
                                out=w1buf[dc % 2][:, 0:P], in_=w13_d[ex, dc * P:(dc + 1) * P,
                                                                     coloff + hc * P: coloff + (hc + 1) * P]))
                            p.bar()
                            for t5 in range(2):
                                p.te(lambda e, part=part, dc=dc, t5=t5, g=g: e.matmul(
                                    (ptile if part == 0 else pmid)[:, t5 * 512:(t5 + 1) * 512],
                                    w1buf[dc % 2][:, 0:P],
                                    xn_fm[dc][:, g * GT + t5 * 512: g * GT + (t5 + 1) * 512],
                                    start=(dc == 0), stop=(dc == NCH - 1)))
                            p.bar()
                    for t5 in range(2):
                        p.se(lambda e, t5=t5: e.activation(gsb[:], ptile[:, t5 * 512:(t5 + 1) * 512], AF.Silu))
                        p.bar()
                        p.ve(lambda e, hc=hc, t5=t5: e.tensor_tensor(
                            hid[hc][:, t5 * 512:(t5 + 1) * 512], pmid[:, t5 * 512:(t5 + 1) * 512],
                            gsb[:], ALU.mult))
                        p.bar()
                for half in range(2):
                    for hc in range(16):
                        p.gp(lambda e, ex=ex, hc=hc, half=half: e.dma_start(
                            out=w2h[hc][:], in_=w2_d[ex, hc * P:(hc + 1) * P, half * 512:(half + 1) * 512]))
                    p.bar()
                    for tg4 in range(2):
                        for t4 in range(4):
                            t = tg4 * 4 + t4
                            pb = ptile[:, t4 * 512:(t4 + 1) * 512]
                            for hc in range(16):
                                p.te(lambda e, pb=pb, hc=hc, t=t: e.matmul(
                                    pb, hid[hc][:, t * P:(t + 1) * P], w2h[hc][:],
                                    start=(hc == 0), stop=(hc == 15)))
                        p.bar()
                        for t4 in range(4):
                            t = tg4 * 4 + t4
                            pb = ptile[:, t4 * 512:(t4 + 1) * 512]
                            tg = g * NT_G + t
                            p.ve(lambda e, pb=pb, t=t, tg=tg, half=half, ex=ex: e.scalar_tensor_tensor(
                                moe[:, t * D + half * 512: t * D + (half + 1) * 512], pb,
                                routes[:, tg * 8 + ex: tg * 8 + ex + 1],
                                moe[:, t * D + half * 512: t * D + (half + 1) * 512],
                                ALU.mult, ALU.add))
                        p.bar()
            for t in range(NT_G):
                tg = g * NT_G + t
                p.sy(lambda e, tg=tg: e.dma_start(out=xt_[:], in_=x_d[tg * P:(tg + 1) * P, :]))
                p.sy(lambda e, tg=tg: e.dma_start(out=st_[:], in_=s_scr[tg * P:(tg + 1) * P, :]))
                p.bar()
                p.se(lambda e, t=t: e.activation(gt_[:], moe[:, t * D:(t + 1) * D], AF.Silu))
                p.bar()
                p.ve(lambda e: e.tensor_tensor(gt_[:], st_[:], gt_[:], ALU.mult))
                p.bar()
                p.ve(lambda e: e.tensor_tensor(gt_[:], xt_[:], gt_[:], ALU.add))
                p.bar()
                p.se_dma(lambda e, tg=tg: e.dma_start(out=xo_d[tg * P:(tg + 1) * P, :], in_=gt_[:]))
                p.bar()

        p.build(block, sem)
    return nc


class MoECtl:
    def __init__(self, swl, spe, sev, sac, sxl, sout):
        self.swl, self.spe, self.sev, self.sac = swl, spe, sev, sac
        self.sxl, self.sout = sxl, sout
        self.c = {"swl": 0, "spe": 0, "sev": 0, "sac": 0, "sxl": 0, "sout": 0}
        self.slab = 0
        self.slab_free_ev = {}


def emit_precast_raw(p, ctl, src_d, dst_d, rows):
    for r in range(ceil_div(rows, P)):
        p.raw("gpsimd", lambda e, r=r: e.dma_start(
            out=dst_d[r * P:(r + 1) * P, :], in_=src_d[r * P:(r + 1) * P, :]).then_inc(ctl.sxl, 16))
        ctl.c["sxl"] += 16


def emit_moe_core(p, ctl, *, xsrc_fm, routes_col, w1bf_d, w2bf_d, b1T_all, b2row,
                  ones_row, zrow_bf, pM, pN, wpp, mid, moe, glu, H1, H2, gsb,
                  pc_at, cmb):
    nq1, nq2 = H1 // 1024, H2 // 1024
    nh1 = H1 // 128
    c = ctl.c

    def load_slab(kind, ex, q):
        sl = ctl.slab
        par = sl % 2
        pcn = pc_at.get((kind, ex))
        if pcn:
            p.raw("sync", lambda e, pcn=pcn: e.wait_ge(ctl.sxl, pcn))
        if sl >= 2:
            need = ctl.slab_free_ev[sl - 2]
            if need:
                p.raw("sync", lambda e, need=need: e.wait_ge(ctl.spe, need))
        for j in range(8):
            if kind == 1:
                src = w1bf_d[ex, j * P:(j + 1) * P, q * 1024:(q + 1) * 1024]
            else:
                src = w2bf_d[ex, (q * 8 + j) * P:(q * 8 + j) * P + P, 0:1024]
            p.raw("sync", lambda e, src=src, par=par, j=j: e.dma_start(
                out=wpp[par][:, j * 1024:(j + 1) * 1024], in_=src).then_inc(ctl.swl, 16))
        c["swl"] += 128
        ctl.slab += 1
        return sl, par, c["swl"]

    moe_reuse_ev = 0
    for g in range(4):
        for ex in range(8):
            bank_ev = {"M": c["sev"], "N": c["sev"]}
            if ex == 0 and moe_reuse_ev:
                bank_ev = dict(bank_ev)
            for q in range(nq1):
                sl, par, wl_need = load_slab(1, ex, q)
                p.raw("tensor", lambda e, wl_need=wl_need: e.wait_ge(ctl.swl, wl_need))
                for half2 in range(2):
                    bank = pM if half2 == 0 else pN
                    key = "M" if half2 == 0 else "N"
                    if bank_ev[key]:
                        p.raw("tensor", lambda e, n=bank_ev[key]: e.wait_ge(ctl.sev, n))
                    for hc4 in range(4):
                        hc = half2 * 4 + hc4
                        for dc in range(NCH):
                            inc = (hc4 == 3 and dc == NCH - 1)
                            p.raw("tensor", lambda e, bank=bank, hc4=hc4, hc=hc, dc=dc,
                                  par=par, g=g, inc=inc: _mm_inc(
                                      e, bank[:, hc4 * 512:(hc4 + 1) * 512],
                                      wpp[par][:, dc * 1024 + hc * P: dc * 1024 + hc * P + P],
                                      xsrc_fm[dc][:, g * 512:(g + 1) * 512],
                                      dc == 0, dc == NCH - 1, ctl.spe if inc else None))
                    c["spe"] += 1
                    spe_need = c["spe"]
                    if not glu:
                        for j4 in range(4):
                            ch = q * 8 + half2 * 4 + j4
                            p.raw("scalar", lambda e, bank=bank, j4=j4, ch=ch, ex=ex,
                                  first=(j4 == 0), spe_need=spe_need: _relu_ev(
                                      e, ctl, mid[:, ch * 512:(ch + 1) * 512],
                                      bank[:, j4 * 512:(j4 + 1) * 512],
                                      b1T_all[:, ex * nh1 + ch: ex * nh1 + ch + 1] if b1T_all is not None else None,
                                      first, spe_need, j4 == 3))
                        c["sev"] += 1
                    else:
                        for pr_ in range(2):
                            p.raw("scalar", lambda e, bank=bank, pr_=pr_, first=(pr_ == 0),
                                  spe_need=spe_need: _silu_ev(
                                      e, ctl, gsb[:, pr_ * 512:(pr_ + 1) * 512],
                                      bank[:, (2 * pr_) * 512:(2 * pr_ + 1) * 512],
                                      first, spe_need, pr_ == 1))
                        c["sac"] += 1
                        sac_need = c["sac"]
                        for pr_ in range(2):
                            hidch = q * 4 + half2 * 2 + pr_
                            p.raw("vector", lambda e, bank=bank, pr_=pr_, hidch=hidch,
                                  first=(pr_ == 0), sac_need=sac_need: _glu_ev(
                                      e, ctl, mid[:, hidch * 512:(hidch + 1) * 512],
                                      bank[:, (2 * pr_ + 1) * 512:(2 * pr_ + 2) * 512],
                                      gsb[:, pr_ * 512:(pr_ + 1) * 512],
                                      first, sac_need, pr_ == 1))
                        c["sev"] += 1
                    bank_ev[key] = c["sev"]
                ctl.slab_free_ev[sl] = c["spe"]
            mid_done = c["sev"]
            p.raw("tensor", lambda e, n=mid_done: e.wait_ge(ctl.sev, n))
            for q2 in range(nq2):
                sl, par, wl_need = load_slab(2, ex, q2)
                p.raw("tensor", lambda e, wl_need=wl_need: e.wait_ge(ctl.swl, wl_need))
                for hb in range(8):
                    hch = q2 * 8 + hb
                    last_acc = (q2 == nq2 - 1 and hb == 7)
                    for t4 in range(4):
                        for half in range(2):
                            bank = (pM if t4 < 2 else pN)[:, ((t4 % 2) * 2 + half) * 512:
                                                          ((t4 % 2) * 2 + half + 1) * 512]
                            stop = last_acc and b2row is None
                            inc = (hb == 7 and t4 == 3 and half == 1)
                            p.raw("tensor", lambda e, bank=bank, hch=hch, t4=t4, half=half,
                                  par=par, hb=hb, stop=stop, inc=inc: _mm_inc(
                                      e, bank, mid[:, hch * 512 + t4 * P: hch * 512 + t4 * P + P],
                                      wpp[par][:, hb * 1024 + half * 512: hb * 1024 + (half + 1) * 512],
                                      hch == 0, stop, ctl.spe if inc else None))
                c["spe"] += 1
                ctl.slab_free_ev[sl] = c["spe"]
            if b2row is not None:
                for t4 in range(4):
                    for half in range(2):
                        bank = (pM if t4 < 2 else pN)[:, ((t4 % 2) * 2 + half) * 512:
                                                      ((t4 % 2) * 2 + half + 1) * 512]
                        inc = (t4 == 3 and half == 1)
                        p.raw("tensor", lambda e, bank=bank, ex=ex, half=half, inc=inc: _mm_inc(
                            e, bank, ones_row[:],
                            b2row[0:1, ex * D + half * 512: ex * D + (half + 1) * 512],
                            False, True, ctl.spe if inc else None))
                c["spe"] += 1
            spe_need = c["spe"]
            for t4 in range(4):
                for half in range(2):
                    bank = (pM if t4 < 2 else pN)[:, ((t4 % 2) * 2 + half) * 512:
                                                  ((t4 % 2) * 2 + half + 1) * 512]
                    tile = g * 4 + t4
                    first = (t4 == 0 and half == 0)
                    last = (t4 == 3 and half == 1)
                    dst = moe[:, t4 * D + half * 512: t4 * D + (half + 1) * 512]
                    p.raw("vector", lambda e, bank=bank, dst=dst, tile=tile, ex=ex,
                          first=first, last=last, spe_need=spe_need,
                          mre=moe_reuse_ev: _route_ev(
                              e, ctl, dst, bank, routes_col(tile, ex), ex == 0,
                              first, spe_need, mre, last))
            c["sev"] += 1
        moe_done = c["sev"]
        moe_reuse_ev = _emit_combine(p, ctl, g, moe_done, moe, cmb, ones_row)
    fin = dict(c)
    def closer(e, f=fin):
        e.wait_ge(ctl.sev, f["sev"])
        e.wait_ge(ctl.swl, f["swl"])
        e.wait_ge(ctl.sxl, f["sxl"])
        e.wait_ge(ctl.sout, f["sout"])
        return e.nop()
    for engn in ("tensor", "vector", "scalar"):
        p.on(engn, closer)
    p.raw("sync", lambda e, f=fin: e.wait_ge(ctl.sout, f["sout"]))
    p.raw("gpsimd", lambda e, f=fin: e.wait_ge(ctl.sout, f["sout"]))
    p.bar()


def _mm_inc(e, out, lhsT, rhs, start, stop, inc_sem):
    inst = e.matmul(out, lhsT, rhs, start=start, stop=stop)
    if inc_sem is not None:
        inst.then_inc(inc_sem, 1)
    return inst


def _relu_ev(e, ctl, dst, bank, bias, first, spe_need, last):
    if first:
        e.wait_ge(ctl.spe, spe_need)
    if bias is not None:
        inst = e.activation(dst, bank, AF.Relu, bias=bias)
    else:
        inst = e.activation(dst, bank, AF.Relu)
    if last:
        inst.then_inc(ctl.sev, 1)
    return inst


def _silu_ev(e, ctl, dst, bank, first, spe_need, last):
    if first:
        e.wait_ge(ctl.spe, spe_need)
    inst = e.activation(dst, bank, AF.Silu)
    if last:
        inst.then_inc(ctl.sac, 1)
    return inst


def _glu_ev(e, ctl, dst, bank, g_sb, first, sac_need, last):
    if first:
        e.wait_ge(ctl.sac, sac_need)
    inst = e.tensor_tensor(dst, bank, g_sb, ALU.mult)
    if last:
        inst.then_inc(ctl.sev, 1)
    return inst


def _route_ev(e, ctl, dst, bank, rcol, is_first_ex, first, spe_need, mre, last):
    if first:
        e.wait_ge(ctl.spe, spe_need)
        if is_first_ex and mre:
            e.wait_ge(ctl.sev, mre)
    if is_first_ex:
        inst = e.tensor_scalar(dst, bank, rcol, None, ALU.mult)
    else:
        inst = e.scalar_tensor_tensor(dst, bank, rcol, dst, ALU.mult, ALU.add)
    if last:
        inst.then_inc(ctl.sev, 1)
    return inst


def _emit_combine(p, ctl, g, moe_done, moe, cmb, ones_row):
    c = ctl.c
    x_d, s_d, xo_d = cmb["x_d"], cmb["s_d"], cmb["xo_d"]
    xt_, st_, gt_ = cmb["xt_"], cmb["st_"], cmb["gt_"]
    if cmb.get("wout") is not None:
        return _emit_combine_wout(p, ctl, g, moe_done, moe, cmb, ones_row)
    ret = None
    for t4 in range(4):
        tg = g * 4 + t4
        prev_dve = c["sev"] if t4 or g else 0
        def ld(e, tg=tg, prev=ctl._cmb_prev if False else None):
            return e
        gate_n = ctl.c.get("cmb_prev_dve", 0)
        p.raw("gpsimd", lambda e, tg=tg, gate_n=gate_n: _cmb_ld(e, ctl, xt_, st_, x_d, s_d, tg, gate_n))
        c["sxl"] += 32
        sxl_need = c["sxl"]
        p.raw("scalar", lambda e, t4=t4, moe_done=moe_done, sxl_need=sxl_need: _cmb_silu(
            e, ctl, gt_, moe[:, t4 * D:(t4 + 1) * D], moe_done, sxl_need))
        c["sev"] += 1
        if t4 == 3:
            ret = c["sev"]
        sev_need = c["sev"]
        p.raw("vector", lambda e, sev_need=sev_need: _cmb_dve(e, ctl, gt_, st_, xt_, sev_need))
        c["sev"] += 1
        ctl.c["cmb_prev_dve"] = c["sev"]
        sev2 = c["sev"]
        p.raw("sync", lambda e, tg=tg, sev2=sev2: _cmb_out(e, ctl, xo_d, gt_, tg, sev2))
        c["sout"] += 16
    return ret


def _cmb_ld(e, ctl, xt_, st_, x_d, s_d, tg, gate_n):
    if gate_n:
        e.wait_ge(ctl.sev, gate_n)
    e.dma_start(out=xt_[:], in_=x_d[tg * P:(tg + 1) * P, :]).then_inc(ctl.sxl, 16)
    return e.dma_start(out=st_[:], in_=s_d[tg * P:(tg + 1) * P, :]).then_inc(ctl.sxl, 16)


def _cmb_silu(e, ctl, gt_, moe_sl, moe_done, sxl_need):
    e.wait_ge(ctl.sev, moe_done)
    e.wait_ge(ctl.sxl, sxl_need)
    return e.activation(gt_[:], moe_sl, AF.Silu).then_inc(ctl.sev, 1)


def _cmb_dve(e, ctl, gt_, st_, xt_, sev_need):
    e.wait_ge(ctl.sev, sev_need)
    e.tensor_tensor(gt_[:], st_[:], gt_[:], ALU.mult)
    return e.tensor_tensor(gt_[:], xt_[:], gt_[:], ALU.add).then_inc(ctl.sev, 1)


def _cmb_out(e, ctl, xo_d, gt_, tg, sev2):
    e.wait_ge(ctl.sev, sev2)
    return e.dma_start(out=xo_d[tg * P:(tg + 1) * P, :], in_=gt_[:]).then_inc(ctl.sout, 16)


def _emit_combine_wout(p, ctl, g, moe_done, moe, cmb, ones_row):
    c = ctl.c
    x_d, s_d, xo_d = cmb["x_d"], cmb["s_d"], cmb["xo_d"]
    xt_, st_, gt_ = cmb["xt_"], cmb["st_"], cmb["gt_"]
    w = cmb["wout"]
    wout_sb, boutrow, moefm, pM, pN, identf = (
        w["wout_sb"], w["boutrow"], w["moefm"], w["pM"], w["pN"], w["identf"])
    base_spe = c["spe"]
    base_sac = c["sac"]
    for k in range(32):
        t4, dc = divmod(k, 8)
        reg = pM[:, (k % 2) * 512:(k % 2) * 512 + P]
        def tr(e, k=k, t4=t4, dc=dc, reg=reg, moe_done=moe_done, base_sac=base_sac):
            if k == 0:
                e.wait_ge(ctl.sev, moe_done)
            if k >= 2:
                e.wait_ge(ctl.sac, base_sac + k - 1)
            return e.transpose(reg, moe[:, t4 * D + dc * P: t4 * D + (dc + 1) * P],
                               identf[:]).then_inc(ctl.spe, 1)
        p.raw("tensor", tr)
        c["spe"] += 1
        def cp(e, k=k, t4=t4, dc=dc, reg=reg, base_spe=base_spe):
            e.wait_ge(ctl.spe, base_spe + k + 1)
            return e.tensor_copy(moefm[dc][:, t4 * P:(t4 + 1) * P], reg).then_inc(ctl.sac, 1)
        p.raw("vector", cp)
        c["sac"] += 1
    ret = c["sac"]
    tr_done_sac = c["sac"]
    def mms(e, tr_done_sac=tr_done_sac):
        e.wait_ge(ctl.sac, tr_done_sac)
        inst = None
        for t4 in range(4):
            for half in range(2):
                bank = (pM if t4 < 2 else pN)[:, ((t4 % 2) * 2 + half) * 512:
                                              ((t4 % 2) * 2 + half + 1) * 512]
                for dc in range(NCH):
                    e.matmul(bank, moefm[dc][:, t4 * P:(t4 + 1) * P],
                             wout_sb[:, dc * D + half * 512: dc * D + (half + 1) * 512],
                             start=(dc == 0), stop=False)
                inst = e.matmul(bank, ones_row[:],
                                boutrow[0:1, half * 512:(half + 1) * 512], start=False, stop=True)
        inst.then_inc(ctl.spe, 1)
        return inst
    p.raw("tensor", mms)
    c["spe"] += 1
    mm_n = c["spe"]
    for t4 in range(4):
        tg = g * 4 + t4
        gate_n = ctl.c.get("cmb_prev_dve", 0)
        p.raw("gpsimd", lambda e, tg=tg, gate_n=gate_n: _cmb_ld(e, ctl, xt_, st_, x_d, s_d, tg, gate_n))
        c["sxl"] += 32
        sxl_need = c["sxl"]
        def sil(e, t4=t4, mm_n=mm_n, sxl_need=sxl_need):
            e.wait_ge(ctl.spe, mm_n)
            e.wait_ge(ctl.sxl, sxl_need)
            inst = None
            for half in range(2):
                bank = (pM if t4 < 2 else pN)[:, ((t4 % 2) * 2 + half) * 512:
                                              ((t4 % 2) * 2 + half + 1) * 512]
                inst = e.activation(gt_[:, half * 512:(half + 1) * 512], bank, AF.Silu)
            inst.then_inc(ctl.sev, 1)
            return inst
        p.raw("scalar", sil)
        c["sev"] += 1
        sev_need = c["sev"]
        p.raw("vector", lambda e, sev_need=sev_need: _cmb_dve(e, ctl, gt_, st_, xt_, sev_need))
        c["sev"] += 1
        ctl.c["cmb_prev_dve"] = c["sev"]
        sev2 = c["sev"]
        p.raw("sync", lambda e, tg=tg, sev2=sev2: _cmb_out(e, ctl, xo_d, gt_, tg, sev2))
        c["sout"] += 16
    def fin(e, tr_done_sac=tr_done_sac):
        e.wait_ge(ctl.sac, tr_done_sac)
        return e.tensor_copy(gt_[:, 0:1], gt_[:, 0:1]).then_inc(ctl.sev, 1)
    p.raw("vector", fin)
    c["sev"] += 1
    return c["sev"]


def build_b1c2():
    nc = bass.Bass("TRN2", target_bir_lowering=False)
    dd = lambda n, s_, dt=F32, k="ExternalInput": nc.dram_tensor(n, s_, dt, kind=k).ap()
    x_d = dd("x", [TPC, D])
    s_in_d = dd("s", [TPC, D])
    xnT_d = dd("xnT", [D, TPC], BF16)
    rt_d = dd("routes", [TPC, 8])
    w1_d = dd("fc1_wT", [8, D, 4096])
    b1T_d = dd("fc1_bT", [8, P, 32])
    w2_d = dd("fc2_w", [8, 4096, D])
    b2_d = dd("fc2_b", [8, D])
    ones_d = dd("ones_row", [1, P], BF16)
    xo_d = dd("xo", [TPC, D], k="ExternalOutput")
    w1bf_d = dd("w1bf", [8, D, 4096], BF16, k="Internal")
    w2bf_d = dd("w2bf", [8, 4096, D], BF16, k="Internal")

    p = Prog()
    with contextlib.ExitStack() as ctx:
        sb = lambda n, s_, dt=F32: ctx.enter_context(nc.sbuf_tensor(n, s_, dt))
        psm = lambda n, s_, dt=F32: ctx.enter_context(nc.psum_tensor(n, s_, dt))
        xn_fm = [sb(f"xnfm{c}", [P, TPC], BF16) for c in range(NCH)]
        mid = sb("mid", [P, 32 * 512], BF16)
        moe = sb("moe", [P, 4 * D])
        wpp = [sb(f"wpp{i}", [P, 8 * 1024], BF16) for i in range(2)]
        routes = sb("routes_s", [P, NT * 8])
        b1T_all = sb("b1T_all", [P, 8 * 32])
        b2row = sb("b2row", [1, 8 * D], BF16)
        ones_row = sb("ones_s", [1, P], BF16)
        gsb = sb("gsb", [P, 2 * 512])
        xt_ = sb("xt_", [P, D])
        st_ = sb("st_", [P, D])
        gt_ = sb("gt_", [P, D])
        pM = psm("pM", [P, 4 * 512])
        pN = psm("pN", [P, 4 * 512])
        sem = ctx.enter_context(nc.semaphore("sem"))
        swl = ctx.enter_context(nc.semaphore("swl"))
        spe = ctx.enter_context(nc.semaphore("spe"))
        sev = ctx.enter_context(nc.semaphore("sev"))
        sac = ctx.enter_context(nc.semaphore("sac"))
        sxl = ctx.enter_context(nc.semaphore("sxl"))
        sout = ctx.enter_context(nc.semaphore("sout"))
        block = ctx.enter_context(nc.Block())
        ctl = MoECtl(swl, spe, sev, sac, sxl, sout)

        for c in range(NCH):
            p.sy(lambda e, c=c: e.dma_start(out=xn_fm[c][:], in_=xnT_d[c * P:(c + 1) * P, :]))
        for t in range(NT):
            p.sy(lambda e, t=t: e.dma_start(out=routes[:, t * 8:(t + 1) * 8], in_=rt_d[t * P:(t + 1) * P, :]))
        for ex in range(8):
            p.sy(lambda e, ex=ex: e.dma_start(out=b1T_all[:, ex * 32:(ex + 1) * 32], in_=b1T_d[ex]))
        p.gp(lambda e: e.dma_start(out=b2row[:], in_=b2_d.rearrange("e d -> (e d)").rearrange("(u f) -> u f", u=1)))
        p.sy(lambda e: e.dma_start(out=ones_row[:], in_=ones_d[:]))
        p.bar()
        pc_at = {}
        for ex in range(8):
            emit_precast_raw(p, ctl, w1_d[ex], w1bf_d[ex], D)
            pc_at[(1, ex)] = ctl.c["sxl"]
            emit_precast_raw(p, ctl, w2_d[ex], w2bf_d[ex], 4096)
            pc_at[(2, ex)] = ctl.c["sxl"]

        def routes_col(tile, ex):
            return routes[:, tile * 8 + ex: tile * 8 + ex + 1]

        emit_moe_core(p, ctl, xsrc_fm=xn_fm, routes_col=routes_col,
                      w1bf_d=w1bf_d, w2bf_d=w2bf_d, b1T_all=b1T_all, b2row=b2row,
                      ones_row=ones_row, zrow_bf=None, pM=pM, pN=pN, wpp=wpp,
                      mid=mid, moe=moe, glu=False, H1=4096, H2=4096, gsb=gsb,
                      pc_at=pc_at,
                      cmb=dict(x_d=x_d, s_d=s_in_d, xo_d=xo_d, xt_=xt_, st_=st_, gt_=gt_))
        p.build(block, sem)
    return nc


def emit_routes_phase(p, lg_src_fm, wg_sb, bg_bc, logits, routes, m1, m2, g1, g2, hsc, pbank):
    for t in range(NT):
        for dc in range(NCH):
            p.te(lambda e, t=t, dc=dc: e.matmul(
                pbank[:, 0:8], lg_src_fm[dc][:, t * P:(t + 1) * P], wg_sb[:, dc * 8:(dc + 1) * 8],
                start=(dc == 0), stop=(dc == NCH - 1)))
        p.bar()
        lg = logits[:, t * 8:(t + 1) * 8]
        if bg_bc is not None:
            p.ve(lambda e, lg=lg: e.tensor_tensor(lg, pbank[:, 0:8], bg_bc[:], ALU.add))
        else:
            p.ve(lambda e, lg=lg: e.tensor_copy(lg, pbank[:, 0:8]))
        p.bar()
        p.ve(lambda e, lg=lg: e.reduce_max(m1[:], lg, AX.X))
        p.bar()
        p.ve(lambda e, lg=lg: e.tensor_single_scalar(hsc[:], lg, m1[:], ALU.is_equal))
        p.bar()
        p.ve(lambda e, lg=lg: e.scalar_tensor_tensor(hsc[:], hsc[:], -1e30, lg, ALU.mult, ALU.add))
        p.bar()
        p.ve(lambda e: e.reduce_max(m2[:], hsc[:], AX.X))
        p.bar()
        p.ve(lambda e: e.tensor_tensor(g1[:], m1[:], m2[:], ALU.subtract))
        p.bar()
        p.se(lambda e: e.activation(g1[:], g1[:], AF.Sigmoid))
        p.bar()
        p.ve(lambda e: e.tensor_scalar(g2[:], g1[:], -1.0, 1.0, ALU.mult, ALU.add))
        p.bar()
        rt = routes[:, t * 8:(t + 1) * 8]
        p.ve(lambda e, rt=rt, lg=lg: e.tensor_single_scalar(rt, lg, m1[:], ALU.is_equal))
        p.bar()
        p.ve(lambda e, rt=rt: e.tensor_scalar(rt, rt, g1[:], None, ALU.mult))
        p.bar()
        p.ve(lambda e: e.tensor_single_scalar(hsc[:], hsc[:], m2[:], ALU.is_equal))
        p.bar()
        p.ve(lambda e, rt=rt: e.scalar_tensor_tensor(hsc[:], hsc[:], g2[:], rt, ALU.mult, ALU.add))
        p.bar()
        p.ve(lambda e, rt=rt: e.tensor_copy(rt, hsc[:]))
        p.bar()


def build_b0c2():
    nc = bass.Bass("TRN2", target_bir_lowering=False)
    dd = lambda n, s_, dt=F32, k="ExternalInput": nc.dram_tensor(n, s_, dt, kind=k).ap()
    x_d = dd("x", [TPC, D])
    s_in_d = dd("s", [TPC, D])
    xnT_d = dd("xnT", [D, TPC], BF16)
    win_d = dd("W_in", [D, D])
    binT_d = dd("b_inT", [P, NCH])
    wg_d = dd("W_gate", [D, 8])
    bg_d = dd("b_gate_bc", [P, 8])
    w1_d = dd("W1", [8, D, 4096])
    b1T_d = dd("b1T", [8, P, 32])
    w2_d = dd("W2", [8, 4096, D])
    b2_d = dd("b2", [8, D])
    wout_d = dd("W_out", [D, D])
    bout_d = dd("b_out", [1, D])
    ones_d = dd("ones_row", [1, P], BF16)
    xo_d = dd("xo", [TPC, D], k="ExternalOutput")
    identf_d = dd("identf", [P, P])
    w1bf_d = dd("w1bf", [8, D, 4096], BF16, k="Internal")
    w2bf_d = dd("w2bf", [8, 4096, D], BF16, k="Internal")

    p = Prog()
    with contextlib.ExitStack() as ctx:
        sb = lambda n, s_, dt=F32: ctx.enter_context(nc.sbuf_tensor(n, s_, dt))
        psm = lambda n, s_, dt=F32: ctx.enter_context(nc.psum_tensor(n, s_, dt))
        h_fm = [sb(f"hfm{c}", [P, TPC], BF16) for c in range(NCH)]
        mid = sb("mid", [P, 32 * 512], BF16)
        moe = sb("moe", [P, 4 * D])
        identf = sb("identf_s", [P, P])
        moefm = [sb(f"moefm{c}", [P, 512], BF16) for c in range(NCH)]
        wpp = [sb(f"wpp{i}", [P, 8 * 1024], BF16) for i in range(2)]
        wout_sb = sb("wout_s", [P, NCH * D], BF16)
        routes = sb("routes_s", [P, NT * 8])
        logits = sb("logits_s", [P, NT * 8])
        wgate = sb("wgate", [P, NCH * 8], BF16)
        bgate = sb("bgate", [P, 8])
        binT = sb("binT_s", [P, NCH])
        b1T_all = sb("b1T_all", [P, 8 * 32])
        b2row = sb("b2row", [1, 8 * D], BF16)
        boutrow = sb("boutrow", [1, D], BF16)
        ones_row = sb("ones_s", [1, P], BF16)
        gsb = sb("gsb", [P, 2 * 512])
        xnbuf = [sb(f"xnbuf{i}", [P, 512], BF16) for i in range(2)]
        m1 = sb("m1", [P, 1]); m2 = sb("m2", [P, 1])
        g1 = sb("g1", [P, 1]); g2 = sb("g2", [P, 1])
        hsc = sb("hsc", [P, 8])
        xt_ = sb("xt_", [P, D])
        st_ = sb("st_", [P, D])
        gt_ = sb("gt_", [P, D])
        pM = psm("pM", [P, 4 * 512])
        pN = psm("pN", [P, 4 * 512])
        sem = ctx.enter_context(nc.semaphore("sem"))
        swl = ctx.enter_context(nc.semaphore("swl"))
        spe = ctx.enter_context(nc.semaphore("spe"))
        sev = ctx.enter_context(nc.semaphore("sev"))
        sac = ctx.enter_context(nc.semaphore("sac"))
        sxl = ctx.enter_context(nc.semaphore("sxl"))
        sout = ctx.enter_context(nc.semaphore("sout"))
        block = ctx.enter_context(nc.Block())
        ctl = MoECtl(swl, spe, sev, sac, sxl, sout)

        p.sy(lambda e: e.dma_start(out=binT[:], in_=binT_d[:]))
        p.sy(lambda e: e.dma_start(out=bgate[:], in_=bg_d[:]))
        p.sy(lambda e: e.dma_start(out=identf[:], in_=identf_d[:]))
        p.sy(lambda e: e.dma_start(out=ones_row[:], in_=ones_d[:]))
        for ex in range(8):
            p.sy(lambda e, ex=ex: e.dma_start(out=b1T_all[:, ex * 32:(ex + 1) * 32], in_=b1T_d[ex]))
        p.gp(lambda e: e.dma_start(out=b2row[:], in_=b2_d.rearrange("e d -> (e d)").rearrange("(u f) -> u f", u=1)))
        p.gp(lambda e: e.dma_start(out=boutrow[:], in_=bout_d[:]))
        emit_cast_load(p, wout_sb, wout_d, D, D)
        emit_cast_load(p, wgate, wg_d, D, 8)
        for dc in range(NCH):
            p.gp(lambda e, dc=dc: e.dma_start(out=wpp[0][:, dc * 1024:(dc + 1) * 1024],
                                              in_=win_d[dc * P:(dc + 1) * P, :]))
        p.bar()

        for tt in range(4):
            for dc in range(NCH):
                p.sy(lambda e, tt=tt, dc=dc: e.dma_start(
                    out=xnbuf[dc % 2][:], in_=xnT_d[dc * P:(dc + 1) * P, tt * 512:(tt + 1) * 512]))
                p.bar()
                for oc in range(NCH):
                    pb = (pM if oc < 4 else pN)[:, (oc % 4) * 512:(oc % 4 + 1) * 512]
                    p.te(lambda e, pb=pb, dc=dc, oc=oc: e.matmul(
                        pb, wpp[0][:, dc * 1024 + oc * P: dc * 1024 + oc * P + P], xnbuf[dc % 2][:],
                        start=(dc == 0), stop=(dc == NCH - 1)))
                p.bar()
            for oc in range(NCH):
                pb = (pM if oc < 4 else pN)[:, (oc % 4) * 512:(oc % 4 + 1) * 512]
                p.ve(lambda e, pb=pb, oc=oc, tt=tt: e.tensor_scalar_add(
                    h_fm[oc][:, tt * 512:(tt + 1) * 512], pb, binT[:, oc:oc + 1]))
            p.bar()

        emit_routes_phase(p, h_fm, wgate, bgate, logits, routes, m1, m2, g1, g2, hsc, pM)

        pc_at = {}
        for ex in range(8):
            emit_precast_raw(p, ctl, w1_d[ex], w1bf_d[ex], D)
            pc_at[(1, ex)] = ctl.c["sxl"]
            emit_precast_raw(p, ctl, w2_d[ex], w2bf_d[ex], 4096)
            pc_at[(2, ex)] = ctl.c["sxl"]

        def routes_col(tile, ex):
            return routes[:, tile * 8 + ex: tile * 8 + ex + 1]

        emit_moe_core(p, ctl, xsrc_fm=h_fm, routes_col=routes_col,
                      w1bf_d=w1bf_d, w2bf_d=w2bf_d, b1T_all=b1T_all, b2row=b2row,
                      ones_row=ones_row, zrow_bf=None, pM=pM, pN=pN, wpp=wpp,
                      mid=mid, moe=moe, glu=False, H1=4096, H2=4096, gsb=gsb,
                      pc_at=pc_at,
                      cmb=dict(x_d=x_d, s_d=s_in_d, xo_d=xo_d, xt_=xt_, st_=st_, gt_=gt_,
                               wout=dict(wout_sb=wout_sb, boutrow=boutrow, moefm=moefm,
                                         pM=pM, pN=pN, identf=identf)))
        p.build(block, sem)
    return nc


def build_b3c2():
    nc = bass.Bass("TRN2", target_bir_lowering=False)
    dd = lambda n, s_, dt=F32, k="ExternalInput": nc.dram_tensor(n, s_, dt, kind=k).ap()
    x_d = dd("x", [TPC, D])
    xnT_d = dd("xnT", [D, TPC], BF16)
    o_dd = dd("o", [TPC, D], BF16)
    gate_dd = dd("gate", [TPC, D], BF16)
    woT_d = dd("WoT", [D, D])
    rtr_d = dd("moe_router", [D, 8])
    w13_d = dd("moe_w13p", [8, D, 4096])
    w2_d = dd("moe_w2", [8, 2048, D])
    ident_d = dd("ident", [P, P], BF16)
    ones_d = dd("ones_row", [1, P], BF16)
    xo_d = dd("xo", [TPC, D], k="ExternalOutput")
    s_scr = dd("s_scr", [TPC, D], k="Internal")
    ogscr_d = dd("ogscr", [TPC, D], BF16, k="Internal")
    w1bf_d = dd("w1bf", [8, D, 4096], BF16, k="Internal")
    w2bf_d = dd("w2bf", [8, 2048, D], BF16, k="Internal")

    p = Prog()
    with contextlib.ExitStack() as ctx:
        sb = lambda n, s_, dt=F32: ctx.enter_context(nc.sbuf_tensor(n, s_, dt))
        psm = lambda n, s_, dt=F32: ctx.enter_context(nc.psum_tensor(n, s_, dt))
        xn_fm = [sb(f"xnfm{c}", [P, TPC], BF16) for c in range(NCH)]
        og_fm = [sb(f"ogfm{c}", [P, TPC], BF16) for c in range(NCH)]
        mid = sb("mid", [P, 16 * 512], BF16)
        moe = sb("moe", [P, 4 * D])
        wpp = [sb(f"wpp{i}", [P, 8 * 1024], BF16) for i in range(2)]
        woT = sb("woT_s", [P, NCH * D], BF16)
        rtr = sb("rtr", [P, NCH * 8], BF16)
        routes = sb("routes_s", [P, NT * 8])
        logits = sb("logits_s", [P, NT * 8])
        m1 = sb("m1", [P, 1]); m2 = sb("m2", [P, 1])
        g1 = sb("g1", [P, 1]); g2 = sb("g2", [P, 1])
        hsc = sb("hsc", [P, 8])
        ident = sb("ident_s", [P, P], BF16)
        ones_row = sb("ones_s", [1, P], BF16)
        ogt = sb("ogt", [P, D], BF16)
        gbf = sb("gbf", [P, D], BF16)
        gsb = sb("gsb", [P, 2 * 512])
        xt_ = sb("xt_", [P, D])
        st_ = sb("st_", [P, D])
        gt_ = sb("gt_", [P, D])
        pM = psm("pM", [P, 4 * 512])
        pN = psm("pN", [P, 4 * 512])
        sem = ctx.enter_context(nc.semaphore("sem"))
        swl = ctx.enter_context(nc.semaphore("swl"))
        spe = ctx.enter_context(nc.semaphore("spe"))
        sev = ctx.enter_context(nc.semaphore("sev"))
        sac = ctx.enter_context(nc.semaphore("sac"))
        sxl = ctx.enter_context(nc.semaphore("sxl"))
        sout = ctx.enter_context(nc.semaphore("sout"))
        block = ctx.enter_context(nc.Block())
        ctl = MoECtl(swl, spe, sev, sac, sxl, sout)

        p.sy(lambda e: e.dma_start(out=ident[:], in_=ident_d[:]))
        for c in range(NCH):
            p.sy(lambda e, c=c: e.dma_start(out=xn_fm[c][:], in_=xnT_d[c * P:(c + 1) * P, :]))
        emit_cast_load(p, woT, woT_d, D, D)
        emit_cast_load(p, rtr, rtr_d, D, 8)
        p.sy(lambda e: e.dma_start(out=ones_row[:], in_=ones_d[:]))
        p.bar()

        for t in range(NT):
            p.sy(lambda e, t=t: e.dma_start(out=ogt[:], in_=o_dd[t * P:(t + 1) * P, :]))
            p.sy(lambda e, t=t: e.dma_start(out=gbf[:], in_=gate_dd[t * P:(t + 1) * P, :]))
            p.bar()
            p.ve(lambda e: e.tensor_tensor(ogt[:], ogt[:], gbf[:], ALU.mult))
            p.bar()
            p.se_dma(lambda e, t=t: e.dma_start(out=ogscr_d[t * P:(t + 1) * P, :], in_=ogt[:]))
            p.bar()
        for c in range(NCH):
            p.sy(lambda e, c=c: e.dma_start_transpose(og_fm[c][:], ogscr_d[:, c * P:(c + 1) * P]))
        p.bar()
        for t in range(NT):
            for half in range(2):
                pb = pM[:, half * 512:(half + 1) * 512]
                for dc in range(NCH):
                    p.te(lambda e, pb=pb, t=t, half=half, dc=dc: e.matmul(
                        pb, og_fm[dc][:, t * P:(t + 1) * P],
                        woT[:, dc * D + half * 512: dc * D + (half + 1) * 512],
                        start=(dc == 0), stop=(dc == NCH - 1)))
            p.bar()
            for half in range(2):
                pb = pM[:, half * 512:(half + 1) * 512]
                p.ve(lambda e, pb=pb, half=half: e.tensor_copy(xt_[:, half * 512:(half + 1) * 512], pb))
            p.bar()
            p.se_dma(lambda e, t=t: e.dma_start(out=s_scr[t * P:(t + 1) * P, :], in_=xt_[:]))
            p.bar()

        emit_routes_phase(p, xn_fm, rtr, None, logits, routes, m1, m2, g1, g2, hsc, pM)

        pc_at = {}
        for ex in range(8):
            emit_precast_raw(p, ctl, w13_d[ex], w1bf_d[ex], D)
            pc_at[(1, ex)] = ctl.c["sxl"]
            emit_precast_raw(p, ctl, w2_d[ex], w2bf_d[ex], 2048)
            pc_at[(2, ex)] = ctl.c["sxl"]

        def routes_col(tile, ex):
            return routes[:, tile * 8 + ex: tile * 8 + ex + 1]

        emit_moe_core(p, ctl, xsrc_fm=xn_fm, routes_col=routes_col,
                      w1bf_d=w1bf_d, w2bf_d=w2bf_d, b1T_all=None, b2row=None,
                      ones_row=ones_row, zrow_bf=None, pM=pM, pN=pN, wpp=wpp,
                      mid=mid, moe=moe, glu=True, H1=4096, H2=2048, gsb=gsb,
                      pc_at=pc_at,
                      cmb=dict(x_d=x_d, s_d=s_scr, xo_d=xo_d, xt_=xt_, st_=st_, gt_=gt_))
        p.build(block, sem)
    return nc


_CACHE = {}


def _get(name, builder):
    if name not in _CACHE:
        _CACHE[name] = builder()
    return _CACHE[name]


def _run(nc, in_maps):
    return run_bass_kernel_spmd(nc, in_maps, core_ids=list(range(NCORE)))


def host_block0(x, W, dbg=None, upto="c"):
    ident = np.eye(P, dtype=BF)
    ones_row = np.ones((1, P), dtype=BF)
    xf = np.ascontiguousarray(x.reshape(TOK, D))

    ncA = _get("b0a", build_b0a)
    maps = []
    for c in range(NCORE):
        maps.append({"x": xf[c * TPC:(c + 1) * TPC], "Wc_down": W["L0_attn_Wc_down"],
                     "Wq_up": W["L0_attn_Wq_up"], "Wkv_up": W["L0_attn_Wkv_up"], "ident": ident})
    ra = _run(ncA, maps).results
    xnT = np.stack([r["xnT"] for r in ra])
    qT = np.stack([r["qT"] for r in ra])
    kv = np.stack([r["kv"] for r in ra])
    if dbg is not None:
        dbg["xnT"], dbg["qT"], dbg["kv"] = xnT, qT, kv

    kv_full = kv.reshape(B, S, 128)
    k_full = kv_full[:, :, :64]
    v_full = kv_full[:, :, 64:]
    kT2 = np.concatenate([k_full.transpose(0, 2, 1)] * 2, axis=1)
    vaug = np.concatenate([v_full.reshape(B, 64, P, 64),
                           np.ones((B, 64, P, 1), dtype=BF)], axis=3)
    qT_b = qT.reshape(2, 4, D, TPC).transpose(0, 2, 1, 3).reshape(2, D, S)
    masks_all = {}
    for i in range(4):
        ms = np.zeros((4, 16, P, 512), dtype=np.float32)
        for sl in range(4):
            gql = L0_TT(i)[sl]
            t_glob = gql * 512 + np.arange(512)[None, :]
            for j in range(16):
                kb = L0_SLOT_EXT[sl] - 16 + j
                s_glob = kb * P + np.arange(P)[:, None]
                ms[sl, j] = np.where(s_glob <= t_glob, 0.0, NEG)
        masks_all[i] = ms.astype(BF)

    ncB = _get("b0b", build_b0b)
    maps = []
    for c in range(NCORE):
        b, i = c // 4, c % 4
        qT_c = np.ascontiguousarray(
            np.concatenate([qT_b[b][:, g * 512:(g + 1) * 512] for g in L0_TT(i)], axis=1))
        maps.append({"qT": qT_c, "kT2": np.ascontiguousarray(kT2[b]),
                     "vaug": np.ascontiguousarray(vaug[b]), "masks": masks_all[i],
                     "Wc_proj": W["L0_attn_Wc_proj"], "ones_row": ones_row})
    rb = _run(ncB, maps).results
    s_full = np.empty((B, S, D), dtype=np.float32)
    for c in range(NCORE):
        b, i = c // 4, c % 4
        sc = rb[c]["s"]
        for sl, g in enumerate(L0_TT(i)):
            s_full[b, g * 512:(g + 1) * 512] = sc[sl * 512:(sl + 1) * 512]
    s_flat = s_full.reshape(TOK, D)
    if dbg is not None:
        dbg["s"] = s_full
    if upto == "b":
        return None

    ncC = _get("b0c2", build_b0c2)
    b_inT = np.ascontiguousarray(W["L0_ffn_b_in"].reshape(NCH, P).T)
    b_gate_bc = np.ascontiguousarray(np.broadcast_to(W["L0_ffn_b_gate"][None, :], (P, 8)))
    b1T = np.ascontiguousarray(W["L0_ffn_b1"].reshape(8, 32, P).transpose(0, 2, 1))
    maps = []
    for c in range(NCORE):
        maps.append({"x": xf[c * TPC:(c + 1) * TPC], "s": s_flat[c * TPC:(c + 1) * TPC],
                     "xnT": xnT[c], "W_in": W["L0_ffn_W_in"], "b_inT": b_inT,
                     "W_gate": W["L0_ffn_W_gate"], "b_gate_bc": b_gate_bc,
                     "W1": W["L0_ffn_W1"], "b1T": b1T, "W2": W["L0_ffn_W2"],
                     "b2": W["L0_ffn_b2"], "W_out": W["L0_ffn_W_out"],
                     "b_out": W["L0_ffn_b_out"][None, :],
                     "identf": np.eye(P, dtype=np.float32), "ones_row": ones_row})
    rc = _run(ncC, maps).results
    x1 = np.concatenate([r["xo"] for r in rc], axis=0)
    return x1.reshape(B, S, D)


def _jshard(x):
    JPC = S // NCORE
    return [np.ascontiguousarray(
        np.concatenate([x[0, c * JPC:(c + 1) * JPC], x[1, c * JPC:(c + 1) * JPC]], axis=0))
        for c in range(NCORE)]


def _junshard(parts):
    JPC = S // NCORE
    out = np.empty((B, S, parts[0].shape[-1]), dtype=parts[0].dtype)
    for c in range(NCORE):
        out[0, c * JPC:(c + 1) * JPC] = parts[c][:JPC]
        out[1, c * JPC:(c + 1) * JPC] = parts[c][JPC:]
    return out


def host_block1(x, W, dbg=None, upto="c"):
    ident = np.eye(P, dtype=BF)
    ones_row = np.ones((1, P), dtype=BF)
    xj = _jshard(x)

    ncA = _get("b1a", build_b1a)
    WqkvT = np.ascontiguousarray(W["L1_attn_Wqkv"].T)
    bqkv_bc = np.ascontiguousarray(np.broadcast_to(W["L1_attn_bqkv"][None, :], (P, 3 * D)))
    WprojT = np.ascontiguousarray(W["L1_attn_Wproj"].T)
    bproj_bc = np.ascontiguousarray(np.broadcast_to(W["L1_attn_bproj"][None, :], (P, D)))
    gwT = np.ascontiguousarray(W["L1_ffn_gate_w"].T)
    maps = [{"x": xj[c], "WqkvT": WqkvT, "bqkv_bc": bqkv_bc, "WprojT": WprojT,
             "bproj_bc": bproj_bc, "gate_wT": gwT, "ident": ident} for c in range(NCORE)]
    ra = _run(ncA, maps).results
    s_j = [r["s"] for r in ra]
    xnT = [r["xnT"] for r in ra]
    logits_j = [r["logits"] for r in ra]
    if dbg is not None:
        dbg["s"] = _junshard(s_j)
        dbg["logits"] = _junshard(logits_j)
    if upto == "a":
        return None

    lg_full = _junshard(logits_j).reshape(TOK, 8)
    lg_re = np.ascontiguousarray(lg_full.reshape(P, P, 8).transpose(1, 0, 2).reshape(P, P * 8))
    tri_i = np.triu(np.ones((P, P))).astype(BF)
    tri_e = np.triu(np.ones((P, P)), k=1).astype(BF)
    aones = np.ones((P, P), dtype=BF)
    ncR = _get("b1rt", build_b1rt)
    rr = _run(ncR, [{"logits_re": lg_re, "tri_incl": tri_i, "tri_excl": tri_e,
                     "allones": aones}] * NCORE).results
    coef_re = rr[0]["coef"]
    coef = coef_re.reshape(P, P, 8).transpose(1, 0, 2).reshape(TOK, 8).reshape(B, S, 8)
    if dbg is not None:
        dbg["coef"] = coef
    if upto == "rt":
        return None

    ncC = _get("b1c2", build_b1c2)
    fc1_wT = np.ascontiguousarray(W["L1_ffn_fc1_w"].transpose(0, 2, 1))
    fc1_bT = np.ascontiguousarray(W["L1_ffn_fc1_b"].reshape(8, 32, P).transpose(0, 2, 1))
    fc2_b = np.ascontiguousarray(W["L1_ffn_fc2_b"].reshape(8, D))
    coef_j = _jshard(coef)
    maps = [{"x": xj[c], "s": s_j[c], "xnT": xnT[c], "routes": coef_j[c],
             "fc1_wT": fc1_wT, "fc1_bT": fc1_bT, "fc2_w": W["L1_ffn_fc2_w"],
             "fc2_b": fc2_b, "ones_row": ones_row} for c in range(NCORE)]
    rc = _run(ncC, maps).results
    return _junshard([r["xo"] for r in rc])


def host_block2(x, W, dbg=None, upto="c"):
    ident = np.eye(P, dtype=BF)
    identf = np.eye(P, dtype=np.float32)
    ones_row = np.ones((1, P), dtype=BF)
    xf = np.ascontiguousarray(x.reshape(TOK, D))

    ncA = _get("b2a", build_b2a)
    qkv_bT = np.ascontiguousarray(W["L2_attn_qkv_b"].reshape(24, P).T)
    rq_bT = np.ascontiguousarray(W["L2_attn_rq_b"].reshape(2, P).T)
    maps = [{"x": xf[c * TPC:(c + 1) * TPC], "qkv_w": W["L2_attn_qkv_w"],
             "qkv_bT": qkv_bT, "rq_w": W["L2_attn_rq_w"], "rq_bT": rq_bT,
             "ident": ident} for c in range(NCORE)]
    ra = _run(ncA, maps).results
    xn = [r["xn"] for r in ra]
    qkvT = [r["qkvT"] for r in ra]
    rqT = [r["rqT"] for r in ra]
    pooledT = [r["pooledT"] for r in ra]
    if dbg is not None:
        dbg["xn"] = np.concatenate(xn, 0).reshape(B, S, D)
        dbg["qkv"] = np.concatenate([q.T for q in qkvT], 0).reshape(B, S, 3 * D)
        dbg["pooled"] = np.concatenate(
            [np.concatenate([pooledT[4 * b + i].T for i in range(4)], 0)[None] for b in range(B)], 0)
    if upto == "a":
        return None

    ncB = _get("b2b", build_b2b)
    NEGf = np.float32(NEG)
    maps = []
    for c in range(NCORE):
        b, i = c // 4, c % 4
        qstart = i * TPC
        qT_c = qkvT[c][0:D]
        kT_c = qkvT[c][D:2 * D]
        vT_c = qkvT[c][2 * D:3 * D]
        if i > 0:
            kT_halo = qkvT[c - 1][D:2 * D, -P:]
            vT_halo = qkvT[c - 1][2 * D:3 * D, -P:]
        else:
            kT_halo = np.zeros((D, P), dtype=BF)
            vT_halo = np.zeros((D, P), dtype=BF)
        kTe = np.ascontiguousarray(np.concatenate([kT_halo, kT_c], axis=1))
        vTe = np.concatenate([vT_halo, vT_c], axis=1)
        v_tm = vTe.reshape(16, 64, 17, P).transpose(0, 2, 3, 1)
        vauge = np.concatenate([v_tm, np.ones((16, 17, P, 1), dtype=BF)], axis=3)
        u = np.arange(P)
        v = np.arange(256)
        masks = np.empty((8, P, 768), dtype=np.float32)
        for qb in range(8):
            S0 = qstart + qb * 256
            for sub in range(3):
                s_glob = qstart + (2 * qb + sub) * P - P + u[:, None]
                t_glob = S0 + v[None, :]
                dist = t_glob - s_glob
                allowed = (dist >= 1) & (dist <= P) & (s_glob >= S0 - 127) & (s_glob >= 0)
                masks[qb, :, sub * 256:(sub + 1) * 256] = np.where(allowed, 0.0, NEGf)
        lmask = np.empty((16, P, 64), dtype=np.float32)
        for t in range(16):
            C_t = qstart // P + t
            lmask[t] = np.where(np.arange(64)[None, :] < C_t, 0.0, NEGf)
        pooled_b = np.ascontiguousarray(
            np.concatenate([pooledT[4 * b + j] for j in range(4)], axis=1))
        glr2 = np.ascontiguousarray(np.concatenate(
            [np.tile(W["L2_attn_gl_w"], (2, 1)), np.tile(W["L2_attn_gr_w"], (2, 1))], axis=1))
        maps.append({
            "qT": np.ascontiguousarray(qT_c), "kTe": kTe,
            "vauge": np.ascontiguousarray(vauge.astype(BF)),
            "masks": masks.astype(BF), "rqT": np.ascontiguousarray(rqT[c]),
            "pooledT_b": pooled_b, "lmask": lmask.astype(BF),
            "rkv_w": W["L2_attn_rkv_w"],
            "rkv_bT": np.ascontiguousarray(W["L2_attn_rkv_b"].reshape(4, P).T),
            "rout_w": W["L2_attn_rout_w"],
            "rout_bT": np.ascontiguousarray(W["L2_attn_rout_b"].reshape(8, P).T),
            "out_w": W["L2_attn_out_w"], "out_b_row": W["L2_attn_out_b"][None, :],
            "glr2": glr2, "ones_row": ones_row, "ident": ident})
    rb = _run(ncB, maps).results
    s_flat = np.concatenate([r["s"] for r in rb], axis=0)
    if dbg is not None:
        dbg["s"] = s_flat.reshape(B, S, D)
    if upto == "b":
        return None

    ncC = _get("b2c", build_b2c)
    maps = [{"x": xf[c * TPC:(c + 1) * TPC], "s": s_flat[c * TPC:(c + 1) * TPC],
             "xn": xn[c], "w_gate": W["L2_ffn_w_gate"], "w_up": W["L2_ffn_w_up"],
             "w_down": W["L2_ffn_w_down"], "ident": ident, "identf": identf}
            for c in range(NCORE)]
    rc = _run(ncC, maps).results
    return np.concatenate([r["xo"] for r in rc], axis=0).reshape(B, S, D)


def host_block3(x, W, dbg=None, upto="c"):
    ident = np.eye(P, dtype=BF)
    ones_row = np.ones((1, P), dtype=BF)
    xf = np.ascontiguousarray(x.reshape(TOK, D))
    w13 = np.asarray(W["L3_ffn_moe_w13"]).reshape(8, D, 2, 16, P)
    w13p = np.ascontiguousarray(w13.transpose(0, 1, 3, 2, 4).reshape(8, D, 4096))

    ncA = _get("b3a", build_b3a)
    trT = lambda w: np.ascontiguousarray(np.asarray(w).T)
    convT = np.concatenate(
        [np.asarray(W[f"L3_attn_conv_{k}"]).reshape(NCH, P, 4).transpose(1, 0, 2).reshape(P, 32)
         for k in ("q", "k", "v")], axis=1)
    sel2 = np.zeros((2, P), dtype=BF)
    sel2[0, 0:64] = 1
    sel2[1, 64:128] = 1
    bd2 = np.ascontiguousarray(sel2.T)
    maps = []
    for c in range(NCORE):
        i = c % 4
        halo = xf[c * TPC - P:c * TPC] if i > 0 else np.zeros((P, D), np.float32)
        x_ext = np.ascontiguousarray(np.concatenate([halo, xf[c * TPC:(c + 1) * TPC]], 0))
        maps.append({"x_ext": x_ext, "WqT": trT(W["L3_attn_Wq"]), "WkT": trT(W["L3_attn_Wk"]),
                     "WvT": trT(W["L3_attn_Wv"]), "WgT": trT(W["L3_attn_Wg"]),
                     "convT": np.ascontiguousarray(convT), "sel2": sel2, "bd2": bd2,
                     "ident": ident})
    ra = _run(ncA, maps).results
    xnT = [r["xnT"] for r in ra]
    gate = [r["gate"] for r in ra]
    if dbg is not None:
        dbg["q"] = np.concatenate([r["qT"].T for r in ra], 0).reshape(B, S, D)
        dbg["k"] = np.concatenate([r["kT"].T for r in ra], 0).reshape(B, S, D)
        dbg["v"] = np.concatenate([r["vT"].T for r in ra], 0).reshape(B, S, D)
        dbg["gate"] = np.concatenate(gate, 0).reshape(B, S, D)
    if upto == "a":
        return None

    qb = [np.concatenate([ra[4 * b + i]["qT"] for i in range(4)], axis=1) for b in range(B)]
    kb = [np.concatenate([ra[4 * b + i]["kT"] for i in range(4)], axis=1) for b in range(B)]
    vb = [np.concatenate([ra[4 * b + i]["vT"] for i in range(4)], axis=1) for b in range(B)]
    ktm_g = [np.ascontiguousarray(kb[b].T) for b in range(B)]
    vtm_g = [np.ascontiguousarray(vb[b].T) for b in range(B)]
    tri_su = np.triu(np.ones((P, P))).astype(BF)
    gam_bc = np.ascontiguousarray(np.broadcast_to(W["L3_attn_ln_gamma"][None, :], (P, 64))).astype(np.float32)
    bet_bc = np.ascontiguousarray(np.broadcast_to(W["L3_attn_ln_beta"][None, :], (P, 64))).astype(np.float32)
    Winit_all = np.asarray(W["L3_attn_W_init"])[0]
    ncB = _get("b3b", build_b3b)
    maps = []
    for c in range(NCORE):
        idxs = [4 * c + k for k in range(4)]
        qT_bh = np.stack([np.ascontiguousarray(qb[idx // 16][64 * (idx % 16):64 * (idx % 16) + 64]) for idx in idxs])
        kT_bh = np.stack([np.ascontiguousarray(kb[idx // 16][64 * (idx % 16):64 * (idx % 16) + 64]) for idx in idxs])
        k_bh = np.stack([np.ascontiguousarray(ktm_g[idx // 16][:, 64 * (idx % 16):64 * (idx % 16) + 64]) for idx in idxs])
        v_bh = np.stack([np.ascontiguousarray(vtm_g[idx // 16][:, 64 * (idx % 16):64 * (idx % 16) + 64]) for idx in idxs])
        Winit = np.stack([np.ascontiguousarray(Winit_all[idx % 16]) for idx in idxs])
        maps.append({"qT_bh": qT_bh, "kT_bh": kT_bh, "k_bh": k_bh, "v_bh": v_bh,
                     "Winit": Winit, "tri_su": tri_su, "gam_bc": gam_bc, "bet_bc": bet_bc})
    rb = _run(ncB, maps).results
    o_g = np.empty((B, S, 16, 64), dtype=BF)
    for c in range(NCORE):
        for k in range(4):
            idx = 4 * c + k
            o_g[idx // 16, :, idx % 16, :] = rb[c]["o_bh"][k]
    o_tm = o_g.reshape(B, S, D).reshape(TOK, D)
    if dbg is not None:
        dbg["o"] = o_g.astype(np.float32).reshape(B, S, D)
    if upto == "b":
        return None

    ncC = _get("b3c2", build_b3c2)
    maps = []
    for c in range(NCORE):
        maps.append({"x": xf[c * TPC:(c + 1) * TPC], "xnT": xnT[c],
                     "o": np.ascontiguousarray(o_tm[c * TPC:(c + 1) * TPC]),
                     "gate": gate[c], "WoT": trT(W["L3_attn_Wo"]),
                     "moe_router": W["L3_ffn_moe_router"], "moe_w13p": w13p,
                     "moe_w2": W["L3_ffn_moe_w2"], "ident": ident,
                     "ones_row": ones_row})
    rc = _run(ncC, maps).results
    return np.concatenate([r["xo"] for r in rc], axis=0).reshape(B, S, D)


def kernel(**inp):
    x = inp["L0_attn_x"]
    x1 = host_block0(np.asarray(x, dtype=np.float32), inp)
    x2 = host_block1(x1, inp)
    x3 = host_block2(x2, inp)
    x4 = host_block3(x3, inp)
    return np.ascontiguousarray(x4.astype(np.float32))
